# version 01
# baseline (speedup 1.0000x reference)
import numpy as np
from contextlib import ExitStack

import concourse.bass as bass
import concourse.mybir as mybir
from concourse.bass_utils import run_bass_kernel_spmd
from concourse.bass import IndirectOffsetOnAxis

F32 = mybir.dt.float32
BF16 = mybir.dt.bfloat16
I32 = mybir.dt.int32
AF = mybir.ActivationFunctionType
ALU = mybir.AluOpType
AX = mybir.AxisListType

NCORE = 8
B, S, D = 4, 4096, 1024
HALF = 2048
NH = 16
HD = 64
BIG = 1.0e9
DEBUG_DUMP = False
DEBUG_DRAM = False
SIM_COMPAT = False

ENG_LIST = ("sync", "vector", "scalar", "tensor", "gpsimd")


class Sem:
    __slots__ = ("h", "n")

    def __init__(self, h):
        self.h = h
        self.n = 0


ZERO_TOK = (None, 0)


class KB:

    def __init__(self, name="prog"):
        self.nc = bass.Bass("TRN2", target_bir_lowering=False, debug=False,
                            detect_race_conditions=False)
        self.ctx = ExitStack()
        self.steps = {e: [] for e in ENG_LIST}
        self.p = {e: Sem(self.ctx.enter_context(self.nc.semaphore(f"p_{e}")))
                  for e in ENG_LIST}
        self.q = {e: Sem(self.ctx.enter_context(self.nc.semaphore(f"q_{e}")))
                  for e in ("sync", "scalar", "gpsimd")}
        self._lastwait = {}
        self.bank = [self.ps(f"bank{i}", [128, 512], F32) for i in range(8)]
        self.bank_free = [ZERO_TOK] * 8

    def sb(self, name, shape, dtype, zero=None):
        stack = getattr(self, "_scope", None) or self.ctx
        t = stack.enter_context(self.nc.sbuf_tensor(name, list(shape), dtype))
        if zero is None:
            zero = stack is not self.ctx
        if zero:
            flat = t.ap()
            if len(shape) == 3:
                flat = flat.rearrange("p a b -> p (a b)")
            tok = self.op("gpsimd", lambda g, flat=flat: g.memset(flat, 0.0))
            for e in ENG_LIST:
                self.wait(e, tok)
        return t

    def open_scope(self):
        assert getattr(self, "_scope", None) is None
        self._scope = ExitStack()

    def close_scope(self):
        self.barrier()
        self._scope.close()
        self._scope = None

    def ps(self, name, shape, dtype=F32):
        return self.ctx.enter_context(self.nc.psum_tensor(name, list(shape), dtype))

    def din(self, name, shape, dtype):
        return self.nc.dram_tensor(name, list(shape), dtype, kind="ExternalInput").ap()

    def dout(self, name, shape, dtype):
        return self.nc.dram_tensor(name, list(shape), dtype, kind="ExternalOutput").ap()

    def dtmp(self, name, shape, dtype):
        kind = "ExternalOutput" if DEBUG_DRAM else "Internal"
        return self.nc.dram_tensor(name, list(shape), dtype, kind=kind).ap()

    def bank_bf(self, i):
        return self.bank[i].ap().bitcast(BF16)

    def raw(self, eng, fn):
        self.steps[eng].append(fn)

    def op(self, eng, fn):
        p = self.p[eng]
        p.n += 1
        n = p.n

        def g(e, fn=fn, h=p.h):
            fn(e).then_inc(h, 1)
        self.steps[eng].append(g)
        return (p, n)

    def wait(self, eng, tok):
        sem, n = tok
        if sem is None or n <= 0:
            return
        key = (eng, id(sem))
        if self._lastwait.get(key, -1) >= n:
            return
        self._lastwait[key] = n
        self.steps[eng].append(lambda e, h=sem.h, n=n: e.wait_ge(h, n))

    def now(self, eng):
        return (self.p[eng], self.p[eng].n)

    def fence(self, eng):
        self.wait(eng, self.now(eng))

    def dma(self, eng, out, in_, **kw):
        q = self.q[eng]
        q.n += 16
        n = q.n

        def g(e, out=out, in_=in_, h=q.h, kw=kw):
            e.dma_start(out=out, in_=in_, **kw).then_inc(h, 16)
        self.steps[eng].append(g)
        return (q, n)

    def use_bank(self, i):
        self.wait("tensor", self.bank_free[i])

    def barrier(self, engines=ENG_LIST):
        toks = [self.now(e) for e in ENG_LIST] + [(self.q[e], self.q[e].n)
                                                 for e in ("sync", "scalar", "gpsimd")]
        for e in engines:
            for t in toks:
                self.wait(e, t)

    def finish(self):
        blk = self.ctx.enter_context(self.nc.Block())
        meth = {"sync": blk.sync, "vector": blk.vector, "scalar": blk.scalar,
                "tensor": blk.tensor, "gpsimd": blk.gpsimd}
        for eng in ENG_LIST:
            fns = self.steps[eng]
            if not fns:
                continue

            def body(e, fns=fns):
                for fn in fns:
                    fn(e)
            meth[eng](body)
        self.ctx.close()
        return self.nc



def emit_consts(k):
    c = {}
    c["id_bf"] = k.sb("c_id_bf", [128, 128], BF16)
    c["id_f"] = k.sb("c_id_f", [128, 128], F32)
    c["tri_bf"] = k.sb("c_tri_bf", [128, 128], BF16)
    c["ones1_f"] = k.sb("c_ones1_f", [1, 512], F32)
    c["ones1_bf"] = k.sb("c_ones1_bf", [1, 512], BF16)
    c["ones16_f"] = k.sb("c_ones16_f", [16, 128], F32)
    c["E16"] = k.sb("c_E16", [16, 16 * 128], F32)

    def seq(fns):
        last = None
        for fn in fns:
            last = k.op("gpsimd", fn)
            k.wait("gpsimd", last)
        return last
    seq([
        lambda g: g.memset(c["id_bf"][:], 1.0),
        lambda g: g.affine_select(out=c["id_bf"][:], in_=c["id_bf"][:], pattern=[[1, 128]],
                                  compare_op=ALU.is_equal, fill=0.0, base=0,
                                  channel_multiplier=-1),
        lambda g: g.memset(c["id_f"][:], 1.0),
        lambda g: g.affine_select(out=c["id_f"][:], in_=c["id_f"][:], pattern=[[1, 128]],
                                  compare_op=ALU.is_equal, fill=0.0, base=0,
                                  channel_multiplier=-1),
        lambda g: g.memset(c["tri_bf"][:], 1.0),
        lambda g: g.affine_select(out=c["tri_bf"][:], in_=c["tri_bf"][:], pattern=[[1, 128]],
                                  compare_op=ALU.is_ge, fill=0.0, base=0,
                                  channel_multiplier=-1),
        lambda g: g.memset(c["ones1_f"][:], 1.0),
        lambda g: g.memset(c["ones1_bf"][:], 1.0),
        lambda g: g.memset(c["ones16_f"][:], 1.0),
        lambda g: g.memset(c["E16"][:], 1.0),
        lambda g: g.affine_select(
            out=c["E16"].ap().rearrange("p (a b) -> p a b", b=128),
            in_=c["E16"].ap().rearrange("p (a b) -> p a b", b=128),
            pattern=[[-1, 16], [0, 128]], compare_op=ALU.is_equal, fill=0.0,
            base=0, channel_multiplier=1),
    ])
    c["ready"] = k.now("gpsimd")
    return c



def emit_ln_xnfm(k, c, x_dram, xn_fm, nt, keep_tm=None, keep_from=0, f32fm_d=None,
                 f32_from=16):
    xt = [k.sb(f"ln_x{i}", [128, D], F32) for i in range(2)]
    xnt = [k.sb(f"ln_xn{i}", [128, D], BF16) for i in range(2)]
    xnf = k.sb("ln_xnf", [128, D], F32) if f32fm_d is not None else None
    sq = k.sb("ln_sq", [128, 1], F32)
    mean = k.sb("ln_mean", [128, 1], F32)
    msq = k.sb("ln_msq", [128, 1], F32)
    rstd = k.sb("ln_rstd", [128, 1], F32)
    scr = k.sb("ln_scr", [128, D], F32)

    consume_tok = [ZERO_TOK, ZERO_TOK]
    rstd_guard = ZERO_TOK
    ld = [None, None]
    ld[0] = k.dma("sync", xt[0][:], x_dram[0:128, :])
    for i in range(nt):
        s = i % 2
        if i + 1 < nt:
            sn = (i + 1) % 2
            k.wait("sync", consume_tok[sn])
            ld[sn] = k.dma("sync", xt[sn][:], x_dram[128 * (i + 1):128 * (i + 2), :])
        k.wait("vector", ld[s])
        k.wait("scalar", ld[s])
        k.op("vector", lambda v, a=xt[s]: v.reduce_sum(mean[:], a[:], axis=AX.X))
        ts_ = k.op("scalar", lambda sc, a=xt[s]: sc.activation(scr[:], a[:], AF.Square,
                                                               accum_out=sq[:]))
        k.fence("vector")
        k.op("vector", lambda v: v.tensor_scalar_mul(mean[:], mean[:], 1.0 / D))
        k.fence("vector")
        k.op("vector", lambda v: v.tensor_tensor(msq[:], mean[:], mean[:], ALU.mult))
        k.fence("vector")
        tv4 = k.op("vector", lambda v: v.tensor_scalar(msq[:], msq[:], -1.0, 1e-5,
                                                       ALU.mult, ALU.add))
        k.wait("scalar", tv4)
        k.wait("scalar", rstd_guard)
        k.op("scalar", lambda sc: sc.activation(rstd[:], sq[:], AF.Ln, bias=msq[:],
                                                scale=1.0 / D))
        k.fence("scalar")
        ts3 = k.op("scalar", lambda sc: sc.activation(rstd[:], rstd[:], AF.Exp, scale=-0.5))
        k.wait("vector", ts3)
        tv5 = k.op("vector", lambda v, a=xt[s], o=xnt[s]: v.tensor_scalar(
            o[:], a[:], mean[:], rstd[:], ALU.subtract, ALU.mult))
        consume_tok[s] = tv5
        rstd_guard = tv5
        if f32fm_d is not None and i >= f32_from:
            k.fence("vector")
            tvf = k.op("vector", lambda v, a=xt[s]: v.tensor_scalar(
                xnf[:], a[:], mean[:], rstd[:], ALU.subtract, ALU.mult))
            consume_tok[s] = tvf
            rstd_guard = tvf
            k.wait("tensor", tvf)
            for j in range(8):
                bk = 6 + (j % 2)
                k.use_bank(bk)
                ttf = k.op("tensor", lambda t, j=j, bk=bk: t.transpose(
                    k.bank[bk][:, 0:128], xnf[:, 128 * j:128 * (j + 1)], c["id_f"][:, :]))
                k.wait("vector", ttf)
                vtf = k.op("vector", lambda v, j=j, bk=bk: v.tensor_copy(
                    xnf[:, 128 * j:128 * (j + 1)], k.bank[bk][:, 0:128]))
                k.bank_free[bk] = vtf
            k.wait("sync", k.now("vector"))
            k.dma("sync", f32fm_d[:, :, 128 * (i - f32_from):128 * (i - f32_from + 1)]
                  .transpose([1, 0, 2]), xnf.ap().rearrange("p (a b) -> p a b", b=128))
        if keep_tm is not None and i >= keep_from:
            k.op("vector", lambda v, o=xnt[s], i=i: v.tensor_copy(
                keep_tm[:, i - keep_from, :], o[:]))
        k.wait("tensor", tv5)
        k.wait("tensor", c["ready"])
        for j in range(8):
            bk = 6 + (j % 2)
            k.use_bank(bk)
            tt = k.op("tensor", lambda t, src=xnt[s], j=j, bk=bk: t.transpose(
                k.bank_bf(bk)[:, 0:128], src[:, 128 * j:128 * (j + 1)], c["id_bf"][:]))
            k.wait("vector", tt)
            tv6 = k.op("vector", lambda v, j=j, i=i, bk=bk: v.tensor_copy(
                xn_fm[:, j, 128 * i:128 * (i + 1)], k.bank_bf(bk)[:, 0:128]))
            k.bank_free[bk] = tv6
    return k.now("vector")



class WStream:

    def __init__(self, k, name, dram, Ktot, blkM, nbuf=2):
        self.k, self.dram = k, dram
        self.kch = Ktot // 128
        self.blkM, self.nbuf = blkM, nbuf
        self.sb = k.sb(name, [128, self.kch, nbuf, blkM], BF16)
        self.tok = [ZERO_TOK] * nbuf
        self.used = [ZERO_TOK] * nbuf

    def load(self, j, ncols=None, dram=None):
        s = j % self.nbuf
        ncols = ncols or self.blkM
        self.k.wait("gpsimd", self.used[s])
        src = (dram if dram is not None else self.dram)[:, j * self.blkM:j * self.blkM + ncols]
        src = src.rearrange("(kc p) m -> p kc m", p=128)
        self.tok[s] = self.k.dma("gpsimd", self.sb[:, :, s, 0:ncols], src)
        return self.tok[s]

    def lhsT(self, j, kc, cols=None):
        return self.sb[:, kc, j % self.nbuf, 0:(cols or self.blkM)]

    def wait_tensor(self, j):
        self.k.wait("tensor", self.tok[j % self.nbuf])

    def mark_used(self, j, tok):
        self.used[j % self.nbuf] = tok



def emit_l0_ffn(k, c, xn_fm, wg_d, wu_d, wd_d, yffn_d):
    HB = 3072
    h_sb = k.sb("l0f_h", [128, 24, 1024], BF16)
    sg = k.sb("l0f_sg", [128, 512], F32)
    yst = k.sb("l0f_y", [128, 512], F32)
    wgS = WStream(k, "l0f_wg", wg_d, D, 128)
    wuS = WStream(k, "l0f_wu", wu_d, D, 128)
    wdS = WStream(k, "l0f_wd", wd_d, HB, 128)

    prev_e2 = ZERO_TOK
    for tb in range(2):
        t0 = HALF + 1024 * tb
        if tb:
            k.wait("vector", k.now("tensor"))
        for f in range(24):
            wgS.load(f)
            wuS.load(f)
            wgS.wait_tensor(f)
            wuS.wait_tensor(f)
            for q in range(2):
                bg = q
                bu = 2 + q
                k.use_bank(bg)
                k.use_bank(bu)
                tcol = t0 + 512 * q
                for kc in range(8):
                    tg = k.op("tensor", lambda t, f=f, kc=kc, bg=bg, tcol=tcol:
                              t.matmul(k.bank[bg][:, :], wgS.lhsT(f, kc),
                                       xn_fm[:, kc, tcol:tcol + 512],
                                       start=(kc == 0), stop=(kc == 7)))
                for kc in range(8):
                    tu = k.op("tensor", lambda t, f=f, kc=kc, bu=bu, tcol=tcol:
                              t.matmul(k.bank[bu][:, :], wuS.lhsT(f, kc),
                                       xn_fm[:, kc, tcol:tcol + 512],
                                       start=(kc == 0), stop=(kc == 7)))
                wgS.mark_used(f, tu)
                wuS.mark_used(f, tu)
                k.wait("scalar", tg)
                k.wait("scalar", prev_e2)
                e1 = k.op("scalar", lambda sc, bg=bg: sc.activation(
                    sg[:], k.bank[bg][:, :], AF.Silu, scale=1.0 / 127.0))
                k.wait("vector", tu)
                k.wait("vector", e1)
                e2 = k.op("vector", lambda v, bu=bu, f=f, q=q: v.scalar_tensor_tensor(
                    h_sb[:, f, 512 * q:512 * (q + 1)], k.bank[bu][:, :], 1.0 / 127.0,
                    sg[:], ALU.mult, ALU.mult))
                k.bank_free[bg] = e1
                k.bank_free[bu] = e2
                prev_e2 = e2
        k.wait("tensor", k.now("vector"))
        for o in range(8):
            wdS.load(o)
            wdS.wait_tensor(o)
            for q in range(2):
                bk = 4 + q
                k.use_bank(bk)
                for kc in range(24):
                    ty = k.op("tensor", lambda t, o=o, kc=kc, bk=bk, q=q:
                              t.matmul(k.bank[bk][:, :], wdS.lhsT(o, kc),
                                       h_sb[:, kc, 512 * q:512 * (q + 1)],
                                       start=(kc == 0), stop=(kc == 23)))
                wdS.mark_used(o, ty)
                k.wait("scalar", ty)
                e3 = k.op("scalar", lambda sc, bk=bk: sc.activation(
                    yst[:], k.bank[bk][:, :], AF.Copy, scale=1.0 / 127.0))
                k.bank_free[bk] = e3
                k.wait("sync", e3)
                st = k.dma("sync", yffn_d[128 * o:128 * (o + 1),
                                          1024 * tb + 512 * q:1024 * tb + 512 * (q + 1)],
                           yst[:])
                k.wait("scalar", st)
    return k.now("sync")



def emit_residual_out(k, c, x_dram, fm_parts, y_out):
    fm_parts = [p if isinstance(p, tuple) else (p, "fm") for p in fm_parts]
    xr = [k.sb(f"res_x{i}", [128, D], F32) for i in range(2)]
    pa = k.sb("res_pa", [128, 8, 128], F32)
    pb = k.sb("res_pb", [128, 8, 128], F32)
    ot = [k.sb(f"res_o{i}", [128, D], F32) for i in range(2)]
    st_tok = [ZERO_TOK, ZERO_TOK]
    use_tok = [ZERO_TOK, ZERO_TOK]
    for i in range(16):
        s = i % 2
        k.wait("sync", use_tok[s])
        k.wait("vector", st_tok[s])
        lx = k.dma("sync", xr[s][:], x_dram[HALF + 128 * i:HALF + 128 * (i + 1), :])
        def _ld(part, dst):
            dram, lay = part
            if lay == "fm":
                return k.dma("scalar", dst.ap(),
                             dram[:, 128 * i:128 * (i + 1)].rearrange("(a p) b -> p a b", p=128))
            return k.dma("scalar", dst.ap().rearrange("p a b -> p (a b)"),
                         dram[128 * i:128 * (i + 1), :])
        la = _ld(fm_parts[0], pa)
        lb = _ld(fm_parts[1], pb)
        k.wait("vector", la)
        k.wait("vector", lb)
        tsum = k.op("vector", lambda v: v.tensor_tensor(
            pa.ap().rearrange("p a b -> p (a b)"), pa.ap().rearrange("p a b -> p (a b)"),
            pb.ap().rearrange("p a b -> p (a b)"), ALU.add))
        k.wait("tensor", tsum)
        k.wait("tensor", c["ready"])
        for j in range(8):
            bk = 6 + (j % 2)
            k.use_bank(bk)
            tt = k.op("tensor", lambda t, j=j, bk=bk: t.transpose(
                k.bank[bk][:, 0:128], pa[:, j, :], c["id_f"][:]))
            k.wait("vector", tt)
            k.wait("vector", lx if j == 0 else ZERO_TOK)
            tv = k.op("vector", lambda v, s=s, j=j, bk=bk: v.tensor_tensor(
                ot[s][:, 128 * j:128 * (j + 1)], k.bank[bk][:, 0:128], xr[s][:, 128 * j:128 * (j + 1)],
                ALU.add))
            k.bank_free[bk] = tv
        use_tok[s] = k.now("vector")
        k.wait("scalar", use_tok[s])
        k.wait("sync", st_tok[s])
        k.wait("sync", use_tok[s])
        st_tok[s] = k.dma("sync", y_out[128 * i:128 * (i + 1), :], ot[s][:])
    k.wait("sync", (k.q["sync"], k.q["sync"].n))
    return k.now("sync")



def emit_l0_attn(k, c, xnfm_d, pqkv_d, pw_d, pe_d, bw_d, be_d, mbig16_d, pwo_d, yattn_d):
    SC = 512
    NSC = S // SC
    NCH_SC = SC // 128
    xn_st = k.sb("a0_xn", [128, 8, SC], BF16)
    Q_hm = k.sb("a0_qhm", [64, NH, SC], BF16)
    K_hm = k.sb("a0_khm", [64, NH, SC], BF16)
    Vf = k.sb("a0_vf", [128, 8, SC], BF16)
    Kf = k.sb("a0_kf", [128, 8, SC], BF16)
    Vaug = k.sb("a0_vaug", [128, NH, 65], BF16)
    Ktm = k.sb("a0_ktm", [128, NH, 64], BF16)
    Ktms = k.sb("a0_ktms", [128, NH, 64], BF16)
    Saug = k.sb("a0_saug", [64, NH, 65], F32)
    Saug_bf = k.sb("a0_saugb", [64, NH, 65], BF16)
    Qs = k.sb("a0_qs", [64, NH, 128], BF16)
    CC = k.sb("a0_C", [16, S], F32)
    Cn = k.sb("a0_Cn", [16, S], F32)
    lnw = k.sb("a0_lnw", [16, SC], F32)
    zsc1 = k.sb("a0_z1", [16, SC], F32)
    zsc2 = k.sb("a0_z2", [16, SC], F32)
    lng = k.sb("a0_lng", [16, SC], F32)
    mb16 = k.sb("a0_mb", [16, SC], F32)
    carry = k.sb("a0_carry", [16, 1], F32)
    C_T = k.sb("a0_ct", [128, 16], F32)
    CnT = k.sb("a0_cnt", [128, 16], F32)
    ceb = [k.sb(f"a0_ceb{i}", [128, 16], F32) for i in range(2)]
    esc = k.sb("a0_esc", [128, 16], F32)
    aq = k.sb("a0_aq", [128, 16], F32)
    aT = k.sb("a0_aT", [16, 128], F32)
    gam = k.sb("a0_gam", [64, 16], F32)
    rhs16 = k.sb("a0_rhs16", [16, 16], F32)
    ef = k.sb("a0_ef", [128, 2048], F32)
    effb = k.sb("a0_effb", [128, NH, 128], BF16)
    den = k.sb("a0_den", [1, 2048], F32)
    rec = k.sb("a0_rec", [1, 2048], F32)
    recb = k.sb("a0_recb", [64, 512], F32)
    xo_fm = k.sb("a0_xo", [128, 8, HALF], BF16)
    stg = k.sb("a0_stg", [128, 512], F32)
    pwS = k.sb("a0_pw", [128, 8, 16], BF16)
    peS = k.sb("a0_pe", [128, 8, 16], BF16)
    bw_s = k.sb("a0_bw", [16, 1], F32)
    be_s = k.sb("a0_be", [16, 1], F32)
    eps8 = k.sb("a0_eps8", [16, 1], F32)
    qkvS = WStream(k, "a0_qkv", pqkv_d, D, 128, nbuf=3)

    z0 = k.op("gpsimd", lambda g: g.memset(Saug.ap().rearrange("p a b -> p (a b)"), 0.0))
    ze8 = k.op("gpsimd", lambda g: g.memset(eps8[:], 1e-8))
    k.wait("scalar", ze8)
    z1 = k.op("gpsimd", lambda g: g.memset(carry[:], 0.0))
    z2 = k.op("gpsimd", lambda g: g.memset(ceb[1][:], 0.0))
    z3 = k.op("gpsimd", lambda g: g.memset(
        Vaug.ap()[:, :, 64:65].rearrange("p a b -> p (a b)"), 1.0))
    for t in (z0, z1, z2, z3):
        k.wait("vector", t)
        k.wait("tensor", t)
    ldw_ = k.dma("gpsimd", pwS.ap(), pw_d.rearrange("(a p) b -> p a b", p=128))
    lde_ = k.dma("gpsimd", peS.ap(), pe_d.rearrange("(a p) b -> p a b", p=128))
    lbw_ = k.dma("sync", bw_s[:], bw_d)
    lbe_ = k.dma("sync", be_s[:], be_d)
    k.wait("tensor", ldw_)
    k.wait("tensor", lde_)
    k.wait("scalar", lbw_)
    k.wait("scalar", lbe_)
    k.wait("vector", lbw_)
    k.wait("vector", lbe_)

    for sc in range(NSC):
        full = (sc >= NSC // 2)
        t0 = SC * sc
        if sc > 0:
            k.wait("sync", k.now("tensor"))
            k.wait("sync", k.now("vector"))
        lx = k.dma("sync", xn_st.ap(),
                   xnfm_d[:, :, t0:t0 + SC].transpose([1, 0, 2]))
        lm = k.dma("sync", mb16[:], mbig16_d[:, t0:t0 + SC])
        k.wait("tensor", lx)
        for (bank0, P) in ((2, pwS), (4, peS)):
            for q in range(2):
                bk = bank0 + q
                k.use_bank(bk)
                for kc in range(8):
                    tz = k.op("tensor", lambda t, P=P, kc=kc, bk=bk, q=q:
                              t.matmul(k.bank[bk][0:16, 0:256], P[:, kc, :],
                                       xn_st[:, kc, 256 * q:256 * (q + 1)],
                                       start=(kc == 0), stop=(kc == 7)))
        blocks = list(range(24)) if full else list(range(8, 24))
        for f in blocks:
            qkvS.load(f)
            qkvS.wait_tensor(f)
            bk = f % 2
            k.use_bank(bk)
            for kc in range(8):
                tm_ = k.op("tensor", lambda t, f=f, kc=kc, bk=bk:
                           t.matmul(k.bank[bk][:, :], qkvS.lhsT(f, kc),
                                    xn_st[:, kc, 0:SC],
                                    start=(kc == 0), stop=(kc == 7)))
            qkvS.mark_used(f, tm_)
            kind = f // 8
            fi = f % 8
            if kind == 2:
                k.wait("vector", tm_)
                ev = k.op("vector", lambda v, fi=fi, bk=bk: v.tensor_copy(
                    Vf[:, fi, :], k.bank[bk][:, :]))
                k.bank_free[bk] = ev
            else:
                k.wait("vector", tm_)
                e1 = k.op("vector", lambda v, bk=bk: v.tensor_scalar_min(
                    stg[:], k.bank[bk][:, :], 0.0))
                k.wait("scalar", e1)
                e2 = k.op("scalar", lambda sc_: sc_.activation(stg[:], stg[:], AF.Exp))
                k.wait("vector", e2)
                dst = Kf if kind == 1 else None
                if kind == 1:
                    e3 = k.op("vector", lambda v, fi=fi, bk=bk: v.scalar_tensor_tensor(
                        Kf[:, fi, :], k.bank[bk][:, :], 0.0, stg[:], ALU.max, ALU.add))
                    k.bank_free[bk] = e3
                    hm = K_hm
                else:
                    e3 = k.op("vector", lambda v, bk=bk: v.scalar_tensor_tensor(
                        stg[:], k.bank[bk][:, :], 0.0, stg[:], ALU.max, ALU.add))
                    k.bank_free[bk] = e3
                    hm = Q_hm
                if kind == 1:
                    src0 = lambda fi=fi: Kf[0:64, fi, :]
                    src1 = lambda fi=fi: Kf[64:128, fi, :]
                else:
                    src0 = lambda: stg[0:64, 0:SC]
                    src1 = lambda: stg[64:128, 0:SC]
                k.fence("vector")
                k.op("vector", lambda v, hm=hm, fi=fi, src0=src0: v.tensor_copy(
                    hm[0:64, 2 * fi, :], src0()))
                e5 = k.op("vector", lambda v, hm=hm, fi=fi, src1=src1: v.tensor_copy(
                    hm[0:64, 2 * fi + 1, :], src1()))
                if kind == 0:
                    k.bank_free[bk] = e5
        k.wait("vector", k.now("tensor"))
        k.wait("scalar", k.now("tensor"))
        k.wait("vector", lm)
        for q in range(2):
            sl = slice(256 * q, 256 * (q + 1))
            bw, be = 2 + q, 4 + q
            a1 = k.op("scalar", lambda sc_, bw=bw, sl=sl: sc_.activation(
                zsc1[0:16, sl], k.bank[bw][0:16, 0:256], AF.Abs, bias=bw_s[:]))
            k.fence("scalar")
            a2 = k.op("scalar", lambda sc_, sl=sl: sc_.activation(
                zsc1[0:16, sl], zsc1[0:16, sl], AF.Exp, scale=-1.0))
            k.fence("scalar")
            a3 = k.op("scalar", lambda sc_, sl=sl: sc_.activation(
                zsc1[0:16, sl], zsc1[0:16, sl], AF.Ln, bias=1.0))
            v1 = k.op("vector", lambda v, bw=bw, sl=sl: v.tensor_scalar(
                zsc2[0:16, sl], k.bank[bw][0:16, 0:256], bw_s[:], 0.0, ALU.add, ALU.min))
            k.wait("vector", a3)
            v2 = k.op("vector", lambda v, sl=sl: v.tensor_tensor(
                lnw[0:16, sl], zsc2[0:16, sl], zsc1[0:16, sl], ALU.subtract))
            k.bank_free[bw] = v2
            a4 = k.op("scalar", lambda sc_, be=be, sl=sl: sc_.activation(
                lng[0:16, sl], k.bank[be][0:16, 0:256], AF.Sigmoid, bias=be_s[:]))
            k.wait("vector", a4)
            v3 = k.op("vector", lambda v, sl=sl: v.tensor_scalar(
                lng[0:16, sl], lng[0:16, sl], -0.95, 0.95, ALU.mult, ALU.add))
            k.wait("scalar", v3)
            a5 = k.op("scalar", lambda sc_, sl=sl: sc_.activation(
                lng[0:16, sl], lng[0:16, sl], AF.Ln, bias=eps8[:]))
            k.bank_free[be] = a5
        if DEBUG_DUMP and sc == 0:
            dbg_lng = k.dtmp("t_dbglng", [16, SC], F32)
            dbg_zw = k.dtmp("t_dbgzw", [16, SC], F32)
            k.barrier()
            k.dma("sync", dbg_lng, lng.ap()[0:16, 0:SC])
            k.dma("sync", dbg_zw, lnw.ap()[0:16, 0:SC])
            k.barrier()
        k.wait("vector", k.now("scalar"))
        k.fence("vector")
        vsc = k.op("vector", lambda v, t0=t0: v.tensor_tensor_scan(
            CC[:, t0:t0 + SC], lng[0:16, 0:SC], lng[0:16, 0:SC], carry[:],
            ALU.add, ALU.bypass))
        k.fence("vector")
        k.op("vector", lambda v, t0=t0: v.tensor_copy(carry[:], CC[:, t0 + SC - 1:t0 + SC]))
        k.op("vector", lambda v, t0=t0: v.tensor_tensor(
            Cn[:, t0:t0 + SC], lnw[0:16, 0:SC], CC[:, t0:t0 + SC], ALU.subtract))
        vcn = k.op("vector", lambda v, t0=t0: v.tensor_tensor(
            Cn[:, t0:t0 + SC], Cn[:, t0:t0 + SC], mb16[0:16, 0:SC], ALU.subtract))
        k.wait("tensor", vcn)

        for cc_i in range(NCH_SC):
            ch = sc * NCH_SC + cc_i
            ct0 = 128 * cc_i
            at0 = t0 + ct0
            cur = ch % 2
            prv = 1 - cur
            k.use_bank(7)
            tA = k.op("tensor", lambda t, at0=at0: t.transpose(
                k.bank[7][:, 0:16], CC[:, at0:at0 + 128], c["id_f"][0:16, 0:16]))
            k.wait("vector", tA)
            vA = k.op("vector", lambda v: v.tensor_copy(C_T[:], k.bank[7][:, 0:16]))
            k.bank_free[7] = vA
            k.use_bank(7)
            tB = k.op("tensor", lambda t, at0=at0: t.transpose(
                k.bank[7][:, 0:16], Cn[:, at0:at0 + 128], c["id_f"][0:16, 0:16]))
            k.wait("vector", tB)
            vB = k.op("vector", lambda v: v.tensor_copy(CnT[:], k.bank[7][:, 0:16]))
            k.bank_free[7] = vB
            vC = k.op("vector", lambda v, at0=at0: v.tensor_scalar_mul(
                rhs16[:], c["id_f"][0:16, 0:16], CC[:, at0 + 127:at0 + 128]))
            k.wait("tensor", vC)
            k.use_bank(7)
            tC = k.op("tensor", lambda t: t.matmul(
                k.bank[7][:, 0:16], c["ones16_f"][:, :], rhs16[:],
                start=True, stop=True))
            k.wait("vector", tC)
            vD = k.op("vector", lambda v, cur=cur: v.tensor_copy(ceb[cur][:], k.bank[7][:, 0:16]))
            k.bank_free[7] = vD
            k.fence("vector")
            k.op("vector", lambda v, cur=cur: v.tensor_tensor(
                esc[:], ceb[cur][:], CnT[:], ALU.add))
            k.wait("scalar", k.now("vector"))
            sE = k.op("scalar", lambda sc_: sc_.activation(esc[:], esc[:], AF.Exp))
            if full:
                vE = k.op("vector", lambda v, prv=prv: v.tensor_tensor(
                    aq[:], C_T[:], ceb[prv][:], ALU.subtract))
                k.wait("scalar", vE)
                sF = k.op("scalar", lambda sc_: sc_.activation(aq[:], aq[:], AF.Exp))
                k.wait("tensor", sF)
                k.use_bank(7)
                tD = k.op("tensor", lambda t: t.transpose(
                    k.bank[7][0:16, 0:128], aq[:], c["id_f"][:, :]))
                k.wait("vector", tD)
                vF = k.op("vector", lambda v: v.tensor_copy(aT[:], k.bank[7][0:16, 0:128]))
                k.bank_free[7] = vF
            k.fence("vector")
            vG = k.op("vector", lambda v, cur=cur, prv=prv: v.tensor_tensor(
                gam[:], ceb[cur][0:64, :], ceb[prv][0:64, :], ALU.subtract))
            k.wait("scalar", vG)
            sG = k.op("scalar", lambda sc_: sc_.activation(gam[:], gam[:], AF.Exp))

            for fi in range(8):
                for (srcbuf, dstbuf) in ((Vf, Vaug), (Kf, Ktm)):
                    bk = 6 if (fi % 2 == 0) else 7
                    k.use_bank(bk)
                    tt = k.op("tensor", lambda t, srcbuf=srcbuf, fi=fi, ct0=ct0, bk=bk:
                              t.transpose(k.bank_bf(bk)[:, 0:128],
                                          srcbuf.ap()[:, fi, ct0:ct0 + 128],
                                          c["id_bf"][:, :]))
                    k.wait("vector", tt)
                    vt = k.op("vector", lambda v, dstbuf=dstbuf, fi=fi, bk=bk: v.tensor_copy(
                        dstbuf.ap()[:, 2 * fi:2 * fi + 2, 0:64],
                        k.bank_bf(bk)[:, 0:128].rearrange("p (a b) -> p a b", b=64)))
                    k.bank_free[bk] = vt
            k.wait("vector", sE)
            vK = k.op("vector", lambda v: v.tensor_tensor(
                Ktms.ap(), Ktm.ap(),
                esc.ap().unsqueeze(2).broadcast_to([128, 16, 64]), ALU.mult))

            if full:
                for h in range(NH):
                    bs = h // 4
                    seg = 128 * (h % 4)
                    k.use_bank(bs)
                    ts1 = k.op("tensor", lambda t, h=h, bs=bs, seg=seg, ct0=ct0:
                               t.matmul(k.bank[bs][:, seg:seg + 128],
                                        K_hm[0:64, h, ct0:ct0 + 128],
                                        Q_hm[0:64, h, ct0:ct0 + 128],
                                        start=True, stop=True))
                for h in range(NH):
                    bd = 4 + h // 4
                    seg = 128 * (h % 4)
                    k.use_bank(bd)
                    k.raw("tensor", lambda t, h=h, bd=bd, seg=seg, at0=at0:
                          t.matmul(k.bank[bd][:, seg:seg + 128],
                                   c["E16"][:, 128 * h:128 * (h + 1)],
                                   CC[:, at0:at0 + 128], start=True, stop=False))
                    ts2 = k.op("tensor", lambda t, h=h, bd=bd, seg=seg, at0=at0:
                               t.matmul(k.bank[bd][:, seg:seg + 128],
                                        Cn[:, at0:at0 + 128],
                                        c["E16"][:, 128 * h:128 * (h + 1)],
                                        start=False, stop=True))
                k.wait("vector", k.now("tensor"))
                for b4 in range(4):
                    vCl = k.op("vector", lambda v, b4=b4: v.tensor_scalar_min(
                        ef[:, 512 * b4:512 * (b4 + 1)], k.bank[4 + b4][:, :], 0.0))
                    k.bank_free[4 + b4] = vCl
                k.wait("scalar", k.now("vector"))
                for b4 in range(4):
                    sX = k.op("scalar", lambda sc_, b4=b4: sc_.activation(
                        ef[:, 512 * b4:512 * (b4 + 1)], ef[:, 512 * b4:512 * (b4 + 1)],
                        AF.Exp))
                k.wait("vector", k.now("scalar"))
                for b4 in range(4):
                    vX = k.op("vector", lambda v, b4=b4: v.tensor_tensor(
                        effb.ap()[:, 4 * b4:4 * (b4 + 1), :].rearrange("p a b -> p (a b)"),
                        ef[:, 512 * b4:512 * (b4 + 1)], k.bank[b4][:, :], ALU.mult))
                    k.bank_free[b4] = vX
                vM = k.op("vector", lambda v: v.tensor_tensor(
                    effb.ap(), effb.ap(),
                    c["tri_bf"].ap().unsqueeze(1).broadcast_to([128, 16, 128]), ALU.mult))
                vSb = k.op("vector", lambda v: v.tensor_copy(Saug_bf.ap(), Saug.ap()))
                k.wait("tensor", vSb)
                for h in range(NH):
                    b6 = 6 if h % 2 == 0 else 7
                    k.use_bank(b6)
                    tq = k.op("tensor", lambda t, h=h, b6=b6: t.matmul(
                        k.bank[b6][0:64, 0:128], c["E16"][:, 128 * h:128 * h + 64],
                        aT[:], start=True, stop=True))
                    k.wait("vector", tq)
                    vq = k.op("vector", lambda v, h=h, b6=b6, ct0=ct0: v.tensor_tensor(
                        Qs[0:64, h, :], Q_hm[0:64, h, ct0:ct0 + 128],
                        k.bank[b6][0:64, 0:128], ALU.mult))
                    k.bank_free[b6] = vq
                k.wait("tensor", vM)
                k.wait("tensor", k.now("vector"))
                for h in range(NH):
                    bs = h // 4
                    seg = 128 * (h % 4)
                    k.use_bank(bs)
                    k.raw("tensor", lambda t, h=h, bs=bs, seg=seg: t.matmul(
                        k.bank[bs][0:65, seg:seg + 128], Vaug[:, h, :],
                        effb[:, h, :], start=True, stop=False))
                    to = k.op("tensor", lambda t, h=h, bs=bs, seg=seg: t.matmul(
                        k.bank[bs][0:65, seg:seg + 128], Saug_bf[0:64, h, :],
                        Qs[0:64, h, :], start=False, stop=True))
            k.wait("vector", sG)
            k.wait("vector", k.now("tensor") if not full else ZERO_TOK)
            vGs = k.op("vector", lambda v: v.tensor_tensor(
                Saug.ap(), Saug.ap(),
                gam.ap().unsqueeze(2).broadcast_to([64, 16, 65]), ALU.mult))
            k.wait("tensor", vK)
            k.wait("tensor", vGs)
            ds_tok = []
            for h in range(NH):
                bd = 4 + h // 7 if not full else 4 + h // 7
                bd = 4 + (h // 7)
                seg = 65 * (h % 7)
                k.use_bank(bd)
                td = k.op("tensor", lambda t, h=h, bd=bd, seg=seg: t.matmul(
                    k.bank[bd][0:64, seg:seg + 65], Ktms[:, h, :],
                    Vaug[:, h, :], start=True, stop=True))
                ds_tok.append((bd, td))
            k.wait("vector", k.now("tensor"))
            for bdi, hs in ((4, (0, 7)), (5, (7, 14)), (6, (14, 16))):
                nseg = hs[1] - hs[0]
                vAdd = k.op("vector", lambda v, bdi=bdi, hs=hs, nseg=nseg: v.tensor_tensor(
                    Saug.ap()[:, hs[0]:hs[1], :].rearrange("p a b -> p (a b)"),
                    Saug.ap()[:, hs[0]:hs[1], :].rearrange("p a b -> p (a b)"),
                    k.bank[bdi].ap()[0:64, 0:65 * nseg], ALU.add))
                k.bank_free[bdi] = vAdd

            if full:
                k.wait("vector", k.now("tensor"))
                for b4 in range(4):
                    k.op("vector", lambda v, b4=b4: v.tensor_copy(
                        den[0:1, 512 * b4:512 * (b4 + 1)], k.bank[b4][64:65, :]))
                k.fence("vector")
                k.op("vector", lambda v: v.tensor_scalar_add(rec[:], den[:], 1e-6))
                k.fence("vector")
                vR = k.op("vector", lambda v: v.reciprocal(rec[:], rec[:]))
                k.wait("tensor", vR)
                for b4 in range(4):
                    k.use_bank(7)
                    tR = k.op("tensor", lambda t, b4=b4: t.matmul(
                        k.bank[7][0:64, 0:512], c["ones1_f"][0:1, 0:64],
                        rec[0:1, 512 * b4:512 * (b4 + 1)], start=True, stop=True))
                    k.wait("vector", tR)
                    vRb = k.op("vector", lambda v: v.tensor_copy(
                        recb[:], k.bank[7][0:64, 0:512]))
                    k.bank_free[7] = vRb
                    for hh in range(4):
                        h = 4 * b4 + hh
                        po = 64 * (h % 2)
                        vO = k.op("vector", lambda v, h=h, hh=hh, b4=b4, po=po, at0=at0:
                                  v.tensor_tensor(
                                      xo_fm[po:po + 64, h // 2, at0 - HALF:at0 - HALF + 128],
                                      k.bank[b4][0:64, 128 * hh:128 * (hh + 1)],
                                      recb[0:64, 128 * hh:128 * (hh + 1)], ALU.mult))
                    k.bank_free[b4] = k.now("vector")
    if DEBUG_DUMP:
        dC = k.dtmp("t_dbgC", [16, S], F32)
        dxo = k.dtmp("t_dbgxo", [128, 8, HALF], BF16)
        k.barrier()
        k.dma("sync", dC, CC.ap())
        k.dma("sync", dxo, xo_fm.ap())
    woS = WStream(k, "a0_wo", pwo_d, D, 128)
    k.wait("tensor", k.now("vector"))
    yst2 = k.sb("a0_yst", [128, 512], F32)
    prev_st = ZERO_TOK
    for o in range(8):
        woS.load(o)
        woS.wait_tensor(o)
        for q in range(4):
            bk = q % 2
            k.use_bank(bk)
            for kc in range(8):
                tw = k.op("tensor", lambda t, o=o, kc=kc, bk=bk, q=q: t.matmul(
                    k.bank[bk][:, :], woS.lhsT(o, kc),
                    xo_fm[:, kc, 512 * q:512 * (q + 1)],
                    start=(kc == 0), stop=(kc == 7)))
            woS.mark_used(o, tw)
            k.wait("scalar", tw)
            k.wait("scalar", prev_st)
            eW = k.op("scalar", lambda sc_, bk=bk: sc_.activation(
                yst2[:], k.bank[bk][:, :], AF.Copy))
            k.bank_free[bk] = eW
            k.wait("sync", eW)
            prev_st = k.dma("sync", yattn_d[128 * o:128 * (o + 1), 512 * q:512 * (q + 1)],
                            yst2[:])
    return k.now("sync")



def build_l0():
    k = KB("l0")
    x_d = k.din("x", [S, D], F32)
    mbig16_d = k.din("mbig16", [16, S], F32)
    pqkv_d = k.din("pqkv", [D, 3 * D], F32)
    pw_d = k.din("pw", [D, 16], F32)
    pe_d = k.din("pe", [D, 16], F32)
    bw_d = k.din("bw", [16, 1], F32)
    be_d = k.din("be", [16, 1], F32)
    pwo_d = k.din("pwo", [D, D], F32)
    wg_d = k.din("wg", [D, 3072], I32)
    wu_d = k.din("wu", [D, 3072], I32)
    wd_d = k.din("wd", [3072, D], I32)
    y_d = k.dout("y", [HALF, D], F32)

    xnfm_d = k.dtmp("t_xnfm", [8, 128, S], BF16)
    yffn_d = k.dtmp("t_yffn", [D, HALF], F32)
    yattn_d = k.dtmp("t_yattn", [D, HALF], F32)

    c = emit_consts(k)
    k.open_scope()
    xn_fm = k.sb("xn_fm", [128, 8, S], BF16)
    emit_ln_xnfm(k, c, x_d, xn_fm, nt=S // 128)
    k.barrier()
    k.dma("sync", xnfm_d.transpose([1, 0, 2]), xn_fm.ap())
    emit_l0_ffn(k, c, xn_fm, wg_d, wu_d, wd_d, yffn_d)
    k.close_scope()
    k.open_scope()
    emit_l0_attn(k, c, xnfm_d, pqkv_d, pw_d, pe_d, bw_d, be_d, mbig16_d, pwo_d, yattn_d)
    k.close_scope()
    k.open_scope()
    emit_residual_out(k, c, x_d, [yffn_d, yattn_d], y_d)
    k.close_scope()
    return k.finish()


def _prep_l0(inputs, core):
    b, h = core // 2, core % 2
    x = inputs["L0_attn_x"][b]
    if h == 0:
        pref = np.zeros((HALF, D), np.float32)
        m = np.concatenate([np.zeros(HALF, np.float32), np.ones(HALF, np.float32)])
        own = x[0:HALF]
    else:
        pref = x[0:HALF]
        m = np.ones(S, np.float32)
        own = x[HALF:S]
    x_eff = np.ascontiguousarray(np.concatenate([pref, own], axis=0), dtype=np.float32)
    mbig16 = np.ascontiguousarray(np.tile((BIG * (1.0 - m))[None, :], (16, 1)),
                                  dtype=np.float32)
    T = lambda a: np.ascontiguousarray(np.asarray(a).T)
    return dict(
        x=x_eff, mbig16=mbig16,
        pqkv=np.ascontiguousarray(np.concatenate(
            [T(inputs["L0_attn_Wq"]), T(inputs["L0_attn_Wk"]), T(inputs["L0_attn_Wv"])],
            axis=1)),
        pw=T(inputs["L0_attn_Ww"]), pe=T(inputs["L0_attn_We"]),
        bw=np.ascontiguousarray(np.asarray(inputs["L0_attn_bw"]).reshape(16, 1),
                                dtype=np.float32),
        be=np.ascontiguousarray(np.asarray(inputs["L0_attn_be"]).reshape(16, 1),
                                dtype=np.float32),
        pwo=T(inputs["L0_attn_Wo"]),
        wg=np.ascontiguousarray(T(inputs["L0_ffn_w_gate"]), dtype=np.int32),
        wu=np.ascontiguousarray(T(inputs["L0_ffn_w_up"]), dtype=np.int32),
        wd=np.ascontiguousarray(T(inputs["L0_ffn_w_down"]), dtype=np.int32),
    )


_BUILDERS = {0: (build_l0, _prep_l0)}


def run_layer(layer, inputs, x_cur):
    build, prep = _BUILDERS[layer]
    nc = build()
    inp = dict(inputs)
    inp[_XKEY[layer]] = x_cur
    in_maps = [prep(inp, core) for core in range(NCORE)]
    res = run_bass_kernel_spmd(nc, in_maps, core_ids=list(range(NCORE)))
    out = np.empty((B, S, D), np.float32)
    for core in range(NCORE):
        b, h = core // 2, core % 2
        out[b, HALF * h:HALF * (h + 1)] = res.results[core]["y"]
    return out


_XKEY = {0: "L0_attn_x", 1: "L0_attn_x", 2: "L0_attn_x", 3: "L0_attn_x"}


def kernel(**inputs):
    inputs = {k_: np.asarray(v) for k_, v in inputs.items()}
    x = inputs["L0_attn_x"].astype(np.float32, copy=False)
    for layer in sorted(_BUILDERS):
        inputs["L0_attn_x"] = x
        x = run_layer(layer, inputs, x)
    return x



def emit_l3_attn(k, c, xnfm_d, pqkv_d, voidk_d, voidv_d, trace_d, tf_d,
                 negmb_col_d, pwo_d, bout_d, yattn_d):
    G = 4
    HPG = 4
    xn_st = k.sb("a3_xn", [128, 8, 1024], BF16)
    K_hm = k.sb("a3_khm", [64, HPG, 4224], BF16)
    Q_hm = k.sb("a3_qhm", [64, HPG, HALF], BF16)
    Vaug = k.sb("a3_vaug", [128, 33, HPG, 65], BF16)
    Vf = k.sb("a3_vf", [128, 2, 1024], BF16)
    stg = k.sb("a3_stg", [128, 512], F32)
    tin = k.sb("a3_tin", [16, 1], F32)
    tcol = k.sb("a3_tcol", [128, 8], F32)
    nmb = k.sb("a3_nmb", [128, 32], F32)
    pt = k.sb("a3_pt", [128, HALF], BF16)
    pvd = k.sb("a3_pvd", [1, HALF], BF16)
    den = k.sb("a3_den", [1, HALF], F32)
    rec = k.sb("a3_rec", [1, HALF], F32)
    recb = k.sb("a3_recb", [64, 512], F32)
    xo_fm = k.sb("a3_xo", [128, 8, HALF], BF16)
    qkvS = WStream(k, "a3_qkv", pqkv_d, D, 128, nbuf=3)

    lt1 = k.dma("sync", tin[:], trace_d)
    lt2 = k.dma("sync", stg[0:16, 0:1], tf_d)
    lnm = k.dma("sync", nmb[:], negmb_col_d)
    k.wait("scalar", lt1)
    k.wait("vector", lt2)
    sT = k.op("scalar", lambda sc_: sc_.activation(tin[:], tin[:], AF.Abs))
    k.wait("vector", sT)
    k.op("vector", lambda v: v.tensor_tensor(tin[:], tin[:], stg[0:16, 0:1], ALU.mult))
    k.fence("vector")
    k.op("vector", lambda v: v.tensor_scalar(tin[:], tin[:], 1.0, 1.0, ALU.add, ALU.max))
    k.fence("vector")
    k.op("vector", lambda v: v.tensor_scalar_mul(tin[:], tin[:], 32.0))
    k.fence("vector")
    vT = k.op("vector", lambda v: v.reciprocal(tin[:], tin[:]))
    k.fence("vector")
    vr16 = k.op("vector", lambda v: v.tensor_scalar_mul(
        stg[0:16, 0:16], c["id_f"][0:16, 0:16], tin[:]))
    k.wait("tensor", vr16)
    for j in range(8):
        k.use_bank(7)
        k.raw("tensor", lambda t, j=j: t.matmul(
            k.bank[7][0:64, 0:1], c["E16"][:, 128 * (2 * j):128 * (2 * j) + 64],
            stg[0:16, 2 * j:2 * j + 1], start=True, stop=True))
        tt2 = k.op("tensor", lambda t, j=j: t.matmul(
            k.bank[7][64:128, 0:1], c["E16"][:, 128 * (2 * j + 1):128 * (2 * j + 1) + 64],
            stg[0:16, 2 * j + 1:2 * j + 2], start=True, stop=True,
            tile_position=(0, 64)))
        k.wait("vector", tt2)
        vt2 = k.op("vector", lambda v, j=j: v.tensor_copy(tcol[:, j:j + 1], k.bank[7][:, 0:1]))
        k.bank_free[7] = vt2
    k.fence("vector")

    for g in range(G):
        if g > 0:
            k.wait("sync", k.now("tensor"))
            k.wait("gpsimd", k.now("tensor"))
        for hh in range(HPG):
            lk_ = k.dma("gpsimd", K_hm[0:64, hh, 4096:4097],
                        voidk_d[HPG * g + hh:HPG * g + hh + 1, :].transpose([1, 0]))
            k.wait("tensor", lk_)
        zo = k.op("gpsimd", lambda g_: g_.memset(
            Vaug.ap()[:, 32, :, :].rearrange("p a b -> p (a b)"), 0.0))
        k.wait("gpsimd", zo)
        z1 = k.op("gpsimd", lambda g_: g_.memset(
            Vaug.ap()[0:1, 32, :, 64:65].rearrange("p a b -> p (a b)"), 1.0))
        k.wait("gpsimd", z1)
        lvv = k.dma("gpsimd", Vaug[0:1, 32, 0:HPG, 0:64],
                    voidv_d[HPG * g:HPG * (g + 1), :].unsqueeze(0))
        for sc in range(4):
            t0 = 1024 * sc
            if sc > 0 or g > 0:
                k.wait("sync", k.now("tensor"))
                k.wait("sync", k.now("vector"))
            lx = k.dma("sync", xn_st.ap(), xnfm_d[:, :, t0:t0 + 1024].transpose([1, 0, 2]))
            k.wait("tensor", lx)
            blocks = [2, 3, 4, 5] if t0 < HALF else [0, 1, 2, 3, 4, 5]
            for f in blocks:
                kind = (0 if f < 2 else (1 if f < 4 else 2))
                fb = 6 * g + f
                qkvS.load(fb)
                qkvS.wait_tensor(fb)
                for q2 in range(2):
                    bk = q2
                    k.use_bank(bk)
                    for kc in range(8):
                        tm_ = k.op("tensor", lambda t, fb=fb, kc=kc, bk=bk, q2=q2:
                                   t.matmul(k.bank[bk][:, :], qkvS.lhsT(fb, kc),
                                            xn_st[:, kc, 512 * q2:512 * (q2 + 1)],
                                            start=(kc == 0), stop=(kc == 7)))
                    qkvS.mark_used(fb, tm_)
                    tcl = 512 * q2 + t0
                    fi = f % 2
                    if kind == 2:
                        k.wait("vector", tm_)
                        ev = k.op("vector", lambda v, fi=fi, bk=bk, q2=q2: v.tensor_copy(
                            Vf[:, fi, 512 * q2:512 * (q2 + 1)], k.bank[bk][:, :]))
                        k.bank_free[bk] = ev
                    elif kind == 1:
                        k.wait("vector", tm_)
                        k.op("vector", lambda v, fi=fi, bk=bk, tcl=tcl: v.tensor_copy(
                            K_hm[0:64, 2 * fi, tcl:tcl + 512], k.bank[bk][0:64, :]))
                        e5 = k.op("vector", lambda v, fi=fi, bk=bk, tcl=tcl: v.tensor_copy(
                            K_hm[0:64, 2 * fi + 1, tcl:tcl + 512], k.bank[bk][64:128, :]))
                        k.bank_free[bk] = e5
                    else:
                        k.wait("vector", tm_)
                        vq1 = k.op("vector", lambda v, bk=bk, g=g, fi=fi: v.tensor_scalar(
                            stg[:], k.bank[bk][:, :], tcol[:, 2 * g + fi:2 * g + fi + 1],
                            None, ALU.mult))
                        k.bank_free[bk] = vq1
                        qcl = tcl - HALF
                        k.op("vector", lambda v, fi=fi, qcl=qcl: v.tensor_copy(
                            Q_hm[0:64, 2 * fi, qcl:qcl + 512], stg[0:64, :]))
                        k.op("vector", lambda v, fi=fi, qcl=qcl: v.tensor_copy(
                            Q_hm[0:64, 2 * fi + 1, qcl:qcl + 512], stg[64:128, :]))
            k.wait("tensor", k.now("vector"))
            for kt8 in range(8):
                kt = 8 * sc + kt8
                for fi in range(2):
                    bk = 6 + (kt8 % 2)
                    k.use_bank(bk)
                    tt = k.op("tensor", lambda t, fi=fi, kt8=kt8, bk=bk: t.transpose(
                        k.bank_bf(bk)[:, 0:128], Vf.ap()[:, fi, 128 * kt8:128 * (kt8 + 1)],
                        c["id_bf"][:, :]))
                    k.wait("vector", tt)
                    vt = k.op("vector", lambda v, fi=fi, kt=kt, bk=bk: v.tensor_copy(
                        Vaug.ap()[:, kt, 2 * fi:2 * fi + 2, 0:64],
                        k.bank_bf(bk)[:, 0:128].rearrange("p (a b) -> p a b", b=64)))
                    k.bank_free[bk] = vt
            zon = k.op("gpsimd", lambda g_, sc=sc: g_.memset(
                Vaug.ap()[:, 8 * sc:8 * (sc + 1), :, 64:65].rearrange("p a c b -> p (a c b)"), 1.0))
            k.wait("tensor", zon)
            k.wait("vector", zon)
        k.wait("tensor", k.now("vector"))
        k.wait("tensor", k.now("gpsimd"))
        for hh in range(HPG):
            h = HPG * g + hh
            for qb in range(4):
                k.use_bank(4 + qb)
            first = True
            for kt in range(33):
                nk = 128 if kt < 32 else 1
                for qb in range(4):
                    k.use_bank(qb)
                    ts1 = k.op("tensor", lambda t, hh=hh, kt=kt, qb=qb, nk=nk: t.matmul(
                        k.bank[qb][0:nk, :], K_hm[0:64, hh, 128 * kt:128 * kt + nk],
                        Q_hm[0:64, hh, 512 * qb:512 * (qb + 1)],
                        start=True, stop=True))
                k.wait("scalar", k.now("tensor"))
                for qb in range(4):
                    if kt < 32:
                        sX = k.op("scalar", lambda sc_, qb=qb, kt=kt: sc_.activation(
                            pt.ap()[:, 512 * qb:512 * (qb + 1)], k.bank[qb][:, :],
                            AF.Exp, bias=nmb[:, kt:kt + 1]))
                    else:
                        sX = k.op("scalar", lambda sc_, qb=qb: sc_.activation(
                            pvd.ap()[0:1, 512 * qb:512 * (qb + 1)], k.bank[qb][0:1, :],
                            AF.Exp))
                    k.bank_free[qb] = sX
                k.wait("tensor", k.now("scalar"))
                for qb in range(4):
                    if kt < 32:
                        tp = k.op("tensor", lambda t, hh=hh, kt=kt, qb=qb, first=first:
                                  t.matmul(k.bank[4 + qb][0:65, :],
                                           Vaug[:, kt, hh, :],
                                           pt.ap()[:, 512 * qb:512 * (qb + 1)],
                                           start=first, stop=False))
                    else:
                        tp = k.op("tensor", lambda t, hh=hh, qb=qb:
                                  t.matmul(k.bank[4 + qb][0:65, :],
                                           Vaug[0:1, 32, hh, :],
                                           pvd.ap()[0:1, 512 * qb:512 * (qb + 1)],
                                           start=False, stop=True))
                first = False
            k.wait("vector", k.now("tensor"))
            for qb in range(4):
                k.op("vector", lambda v, qb=qb: v.tensor_copy(
                    den[0:1, 512 * qb:512 * (qb + 1)], k.bank[4 + qb][64:65, :]))
            k.fence("vector")
            vR = k.op("vector", lambda v: v.reciprocal(rec[:], den[:]))
            k.wait("tensor", vR)
            for qb in range(4):
                k.use_bank(0)
                tR = k.op("tensor", lambda t, qb=qb: t.matmul(
                    k.bank[0][0:64, 0:512], c["ones1_f"][0:1, 0:64],
                    rec[0:1, 512 * qb:512 * (qb + 1)], start=True, stop=True))
                k.wait("vector", tR)
                vRb = k.op("vector", lambda v: v.tensor_copy(recb[:], k.bank[0][0:64, 0:512]))
                k.bank_free[0] = vRb
                po = 64 * (h % 2)
                vO = k.op("vector", lambda v, h=h, qb=qb, po=po: v.tensor_tensor(
                    xo_fm[po:po + 64, h // 2, 512 * qb:512 * (qb + 1)],
                    k.bank[4 + qb][0:64, :], recb[:], ALU.mult))
                k.bank_free[4 + qb] = vO
    woS = WStream(k, "a3_wo", pwo_d, D, 128)
    bo = k.sb("a3_bo", [128, 8], F32)
    lb = k.dma("sync", bo[:], bout_d)
    k.wait("scalar", lb)
    k.wait("tensor", k.now("vector"))
    yst3 = k.sb("a3_yst", [128, 512], F32)
    prev_st = ZERO_TOK
    for o in range(8):
        woS.load(o)
        woS.wait_tensor(o)
        for q in range(4):
            bk = q % 2
            k.use_bank(bk)
            for kc in range(8):
                tw = k.op("tensor", lambda t, o=o, kc=kc, bk=bk, q=q: t.matmul(
                    k.bank[bk][:, :], woS.lhsT(o, kc),
                    xo_fm[:, kc, 512 * q:512 * (q + 1)],
                    start=(kc == 0), stop=(kc == 7)))
            woS.mark_used(o, tw)
            k.wait("scalar", tw)
            k.wait("scalar", prev_st)
            eW = k.op("scalar", lambda sc_, bk=bk, o=o: sc_.activation(
                yst3[:], k.bank[bk][:, :], AF.Identity, bias=bo[:, o:o + 1]))
            k.bank_free[bk] = eW
            k.wait("sync", eW)
            prev_st = k.dma("sync", yattn_d[128 * o:128 * (o + 1), 512 * q:512 * (q + 1)],
                            yst3[:])
    return k.now("sync")



def alloc_mxq(k, tag):
    return {"am": k.sb(f"{tag}_am", [128, 32], F32),
            "rs": k.sb(f"{tag}_rs", [128, 32], F32),
            "y": k.sb(f"{tag}_y", [128, 1024], F32),
            "ay": k.sb(f"{tag}_ay", [128, 1024], F32),
            "m2": k.sb(f"{tag}_m2", [128, 1024], F32),
            "qi": k.sb(f"{tag}_qi", [128, 1024], I32)}


def emit_mxq(k, m, src_ap, dst_ap):
    k.op("vector", lambda v: v.tensor_reduce(
        m["am"].ap(), src_ap.rearrange("p (a b) -> p a b", b=32), axis=AX.X,
        op=ALU.max, apply_absolute_value=True))
    k.fence("vector")
    k.op("vector", lambda v: v.tensor_scalar(m["am"].ap(), m["am"].ap(), 1.0 / 7.5,
                                             1e-30, ALU.mult, ALU.max))
    k.fence("vector")
    k.op("vector", lambda v: v.reciprocal(m["rs"].ap(), m["am"].ap()))
    k.fence("vector")
    t2 = k.op("vector", lambda v: v.tensor_tensor(
        m["y"].ap().rearrange("p (a b) -> p a b", b=32),
        src_ap.rearrange("p (a b) -> p a b", b=32),
        m["rs"].ap().unsqueeze(2).broadcast_to([128, 32, 32]), ALU.mult))
    k.wait("scalar", t2)
    sA = k.op("scalar", lambda sc_: sc_.activation(m["ay"].ap(), m["y"].ap(), AF.Abs))
    k.wait("vector", sA)
    k.op("vector", lambda v: v.tensor_scalar(m["m2"].ap(), m["ay"].ap(), 2.0, None, ALU.is_ge))
    k.fence("vector")
    k.op("vector", lambda v: v.tensor_scalar(m["ay"].ap(), m["ay"].ap(), 4.0, None, ALU.is_ge))
    k.fence("vector")
    k.op("vector", lambda v: v.tensor_scalar_add(m["m2"].ap(), m["m2"].ap(), 1.0))
    k.op("vector", lambda v: v.tensor_scalar_add(m["ay"].ap(), m["ay"].ap(), 1.0))
    k.fence("vector")
    k.op("vector", lambda v: v.tensor_tensor(m["m2"].ap(), m["m2"].ap(), m["ay"].ap(),
                                             ALU.mult))
    k.fence("vector")
    k.op("vector", lambda v: v.reciprocal(m["ay"].ap(), m["m2"].ap()))
    k.fence("vector")
    k.op("vector", lambda v: v.tensor_tensor(m["y"].ap(), m["y"].ap(), m["ay"].ap(),
                                             ALU.mult))
    k.fence("vector")
    ty = k.op("vector", lambda v: v.tensor_scalar_mul(m["y"].ap(), m["y"].ap(), 8.0))
    if SIM_COMPAT:
        k.wait("scalar", ty)
        sS = k.op("scalar", lambda sc_: sc_.activation(m["ay"].ap(), m["y"].ap(), AF.Sign))
        k.wait("vector", sS)
        k.op("vector", lambda v: v.scalar_tensor_tensor(m["y"].ap(), m["ay"].ap(), 0.5,
                                                        m["y"].ap(), ALU.mult, ALU.add))
        k.fence("vector")
    k.op("vector", lambda v: v.tensor_copy(m["qi"].ap(), m["y"].ap()))
    k.fence("vector")
    k.op("vector", lambda v: v.tensor_copy(m["y"].ap(), m["qi"].ap()))
    k.fence("vector")
    k.op("vector", lambda v: v.tensor_tensor(m["y"].ap(), m["y"].ap(), m["m2"].ap(),
                                             ALU.mult))
    k.fence("vector")
    k.op("vector", lambda v: v.tensor_scalar_mul(m["y"].ap(), m["y"].ap(), 0.125))
    k.fence("vector")
    tl = k.op("vector", lambda v: v.tensor_tensor(
        dst_ap.rearrange("p (a b) -> p a b", b=32),
        m["y"].ap().rearrange("p (a b) -> p a b", b=32),
        m["am"].ap().unsqueeze(2).broadcast_to([128, 32, 32]), ALU.mult))
    return tl


def emit_l3_ffn(k, c, xn_tm, wraw_d, bias_d, yffn_d):
    m = alloc_mxq(k, "f3")
    wq_om = k.sb("f3_wom", [128, 8, 1024], BF16)
    wq_fm = k.sb("f3_wfm", [128, 8, 1024], BF16)
    xq_tm = k.sb("f3_xqt", [128, 1024], BF16)
    xq_fm = k.sb("f3_xq", [128, 8, HALF], BF16)
    wst = m["ay"]
    bia = k.sb("f3_bias", [128, 8], F32)
    yst = None
    lb = k.dma("sync", bia[:], bias_d)
    k.wait("scalar", lb)
    prev = ZERO_TOK
    for o in range(8):
        k.wait("sync", prev)
        lw = k.dma("sync", wst[:], wraw_d[128 * o:128 * (o + 1), :])
        k.wait("vector", lw)
        k.wait("scalar", lw)
        prev = emit_mxq(k, m, wst.ap(), wq_om.ap()[:, o, :])
    k.wait("tensor", prev)
    for o in range(8):
        for i in range(8):
            bk = 6 + (i % 2)
            k.use_bank(bk)
            tt = k.op("tensor", lambda t, o=o, i=i, bk=bk: t.transpose(
                k.bank_bf(bk)[:, 0:128], wq_om.ap()[:, o, 128 * i:128 * (i + 1)],
                c["id_bf"][:, :]))
            k.wait("vector", tt)
            vt = k.op("vector", lambda v, o=o, i=i, bk=bk: v.tensor_copy(
                wq_fm[:, i, 128 * o:128 * (o + 1)], k.bank_bf(bk)[:, 0:128]))
            k.bank_free[bk] = vt
    for i in range(16):
        tq = emit_mxq(k, m, xn_tm.ap()[:, i, :], xq_tm.ap())
        k.wait("tensor", tq)
        for j in range(8):
            bk = 6 + (j % 2)
            k.use_bank(bk)
            tt = k.op("tensor", lambda t, j=j, bk=bk: t.transpose(
                k.bank_bf(bk)[:, 0:128], xq_tm.ap()[:, 128 * j:128 * (j + 1)],
                c["id_bf"][:, :]))
            k.wait("vector", tt)
            vt = k.op("vector", lambda v, i=i, j=j, bk=bk: v.tensor_copy(
                xq_fm[:, j, 128 * i:128 * (i + 1)], k.bank_bf(bk)[:, 0:128]))
            k.bank_free[bk] = vt
    yst = m["y"]
    k.wait("tensor", k.now("vector"))
    prev_st = ZERO_TOK
    for o in range(8):
        for q in range(4):
            bk = q % 2
            k.use_bank(bk)
            for kc in range(8):
                tw = k.op("tensor", lambda t, o=o, kc=kc, bk=bk, q=q: t.matmul(
                    k.bank[bk][:, :], wq_fm[:, kc, 128 * o:128 * (o + 1)],
                    xq_fm[:, kc, 512 * q:512 * (q + 1)],
                    start=(kc == 0), stop=(kc == 7)))
            k.wait("scalar", tw)
            k.wait("scalar", prev_st)
            eW = k.op("scalar", lambda sc_, bk=bk, o=o: sc_.activation(
                yst[:, 0:512], k.bank[bk][:, :], AF.Identity, bias=bia[:, o:o + 1]))
            k.bank_free[bk] = eW
            k.wait("sync", eW)
            prev_st = k.dma("sync", yffn_d[128 * o:128 * (o + 1), 512 * q:512 * (q + 1)],
                            yst[:, 0:512])
    return k.now("sync")


def build_l3():
    k = KB("l3")
    x_d = k.din("x", [S, D], F32)
    negmb_d = k.din("negmb", [128, 32], F32)
    pqkv_d = k.din("pqkv", [D, 3 * D], F32)
    voidk_d = k.din("voidk", [16, 64], F32)
    voidv_d = k.din("voidv", [16, 64], F32)
    trace_d = k.din("trace", [16, 1], F32)
    tf_d = k.din("tf", [16, 1], F32)
    pwo_d = k.din("pwo", [D, D], F32)
    bout_d = k.din("bout", [128, 8], F32)
    wffn_d = k.din("wffn", [D, D], F32)
    bffn_d = k.din("bffn", [128, 8], F32)
    y_d = k.dout("y", [HALF, D], F32)

    xnfm_d = k.dtmp("t_xnfm", [8, 128, S], BF16)
    yffn_d = k.dtmp("t_yffn", [D, HALF], F32)
    yattn_d = k.dtmp("t_yattn", [D, HALF], F32)

    c = emit_consts(k)
    k.open_scope()
    xn_fm = k.sb("xn_fm", [128, 8, S], BF16)
    xn_tm = k.sb("xn_tm", [128, 16, D], BF16)
    emit_ln_xnfm(k, c, x_d, xn_fm, nt=S // 128, keep_tm=xn_tm, keep_from=16)
    k.barrier()
    k.dma("sync", xnfm_d.transpose([1, 0, 2]), xn_fm.ap())
    emit_l3_ffn(k, c, xn_tm, wffn_d, bffn_d, yffn_d)
    k.close_scope()
    k.open_scope()
    emit_l3_attn(k, c, xnfm_d, pqkv_d, voidk_d, voidv_d, trace_d, tf_d,
                 negmb_d, pwo_d, bout_d, yattn_d)
    k.close_scope()
    k.open_scope()
    emit_residual_out(k, c, x_d, [yffn_d, yattn_d], y_d)
    k.close_scope()
    return k.finish()


def _prep_common_x(inputs, core, xkey="L0_attn_x"):
    b, h = core // 2, core % 2
    x = inputs[xkey][b]
    if h == 0:
        pref = np.zeros((HALF, D), np.float32)
        m = np.concatenate([np.zeros(HALF, np.float32), np.ones(HALF, np.float32)])
        own = x[0:HALF]
    else:
        pref = x[0:HALF]
        m = np.ones(S, np.float32)
        own = x[HALF:S]
    x_eff = np.ascontiguousarray(np.concatenate([pref, own], axis=0), dtype=np.float32)
    return x_eff, m


def _prep_l3(inputs, core):
    b, h = core // 2, core % 2
    x = inputs["L0_attn_x"][b]
    other = x[HALF:S] if h == 0 else x[0:HALF]
    own = x[0:HALF] if h == 0 else x[HALF:S]
    x_eff = np.ascontiguousarray(np.concatenate([other, own], axis=0), dtype=np.float32)
    m = np.ones(S, np.float32)
    W = inputs["L3_attn_w_qkv"]
    blocks = []
    for g in range(4):
        blocks += [W[:, 256 * g:256 * (g + 1)],
                   W[:, 1024 + 256 * g:1024 + 256 * (g + 1)],
                   W[:, 2048 + 256 * g:2048 + 256 * (g + 1)]]
    pqkv = np.ascontiguousarray(np.concatenate(blocks, axis=1), dtype=np.float32)
    A = np.ascontiguousarray
    return dict(
        x=x_eff,
        negmb=A((-BIG * (1.0 - m)).reshape(32, 128).T, dtype=np.float32),
        pqkv=pqkv,
        voidk=A(np.asarray(inputs["L3_attn_void_k"])[0, :, 0, :], dtype=np.float32),
        voidv=A(np.asarray(inputs["L3_attn_void_v"])[0, :, 0, :], dtype=np.float32),
        trace=A(np.asarray(inputs["L3_attn_attention_trace"]).reshape(16, 1),
                dtype=np.float32),
        tf=A(np.asarray(inputs["L3_attn_temperature_factor"]).reshape(16, 1),
             dtype=np.float32),
        pwo=A(np.asarray(inputs["L3_attn_w_out"]), dtype=np.float32),
        bout=A(np.asarray(inputs["L3_attn_b_out"]).reshape(8, 128).T, dtype=np.float32),
        wffn=A(np.asarray(inputs["L3_ffn_weight"]), dtype=np.float32),
        bffn=A(np.asarray(inputs["L3_ffn_bias"]).reshape(8, 128).T, dtype=np.float32),
    )


_BUILDERS[3] = (build_l3, _prep_l3)



def emit_moe(k, c, tag, xnfm_d, xntm_d, acc_d, idxgv_d, E, C, mode, act_fn,
             wr_d, br_d, w1_d, b1c_d, w2_d, b2c_d, F=4096):
    NB = C // 128
    NT = 16
    FC = F // 128
    xl = k.sb(f"{tag}_xl", [128, 8, 128], F32)
    wr_s = k.sb(f"{tag}_wr", [128, 8, E], F32)
    br_s = k.sb(f"{tag}_br", [1, E], F32)
    run = k.sb(f"{tag}_run", [1, E], F32)
    lg = k.sb(f"{tag}_lg", [128, E], F32)
    gate = k.sb(f"{tag}_gate", [128, E], F32)
    mask = k.sb(f"{tag}_mask", [128, E], F32)
    m1 = k.sb(f"{tag}_m1", [128, 1], F32)
    m2 = k.sb(f"{tag}_m2", [128, 1], F32)
    scr = k.sb(f"{tag}_scr", [128, E], F32)
    tris = k.sb(f"{tag}_tris", [128, 128], F32)
    onesc = k.sb(f"{tag}_onesc", [128, 1], F32)
    slot_i = k.sb(f"{tag}_sloti", [128, E], I32)
    pairs = k.sb(f"{tag}_pairs", [128, E, 2], F32)
    iota0 = k.sb(f"{tag}_iota", [128, 1], I32)
    tokf = k.sb(f"{tag}_tokf", [128, 1], F32)
    ei = k.sb(f"{tag}_ei", [1, E], I32)
    obb = k.sb(f"{tag}_obb", [128, 128], F32)
    zt = k.sb(f"{tag}_zt", [128, 1024], F32)
    ib = k.sb(f"{tag}_ib", [128, 2 * NB], F32)
    idxc = k.sb(f"{tag}_idxc", [128, NB], I32)
    xg_tm = k.sb(f"{tag}_xgtm", [128, 1024], BF16)
    xg_fm = k.sb(f"{tag}_xgfm", [128, 8, C], BF16)
    h_sb = k.sb(f"{tag}_h", [128, FC, C], BF16)
    b1s = k.sb(f"{tag}_b1s", [128, FC], F32)
    b2s = k.sb(f"{tag}_b2s", [128, 8], F32)
    y_fm = k.sb(f"{tag}_yfm", [128, 8, C], F32)
    y_tm = k.sb(f"{tag}_ytm", [128, 1024], F32)

    def idma_scatter(out_dram, off_col, in_ap):
        q = k.q["gpsimd"]
        q.n += 16
        n = q.n

        def gfn(g_, out_dram=out_dram, off_col=off_col, in_ap=in_ap, h=q.h):
            g_.indirect_dma_start(out=out_dram,
                                  out_offset=IndirectOffsetOnAxis(off_col, axis=0),
                                  in_=in_ap, in_offset=None,
                                  compute_op=ALU.bypass).then_inc(h, 16)
        k.steps["gpsimd"].append(gfn)
        return (q, n)

    def idma_scatter_add(out_dram, off_col, in_ap):
        q = k.q["gpsimd"]
        q.n += 16
        n = q.n

        def gfn(g_, out_dram=out_dram, off_col=off_col, in_ap=in_ap, h=q.h):
            g_.indirect_dma_start(out=out_dram,
                                  out_offset=IndirectOffsetOnAxis(off_col, axis=0),
                                  in_=in_ap, in_offset=None,
                                  compute_op=ALU.add).then_inc(h, 16)
        k.steps["gpsimd"].append(gfn)
        return (q, n)

    def idma_gather(out_ap, in_dram, off_col):
        q = k.q["gpsimd"]
        q.n += 16
        n = q.n

        def gfn(g_, out_ap=out_ap, in_dram=in_dram, off_col=off_col, h=q.h):
            g_.indirect_dma_start(out=out_ap, out_offset=None, in_=in_dram,
                                  in_offset=IndirectOffsetOnAxis(off_col, axis=0)
                                  ).then_inc(h, 16)
        k.steps["gpsimd"].append(gfn)
        return (q, n)

    g0 = k.op("gpsimd", lambda g: g.memset(tris[:], 1.0))
    k.wait("gpsimd", g0)
    g1 = k.op("gpsimd", lambda g: g.affine_select(
        out=tris[:], in_=tris[:], pattern=[[1, 128]], compare_op=ALU.is_ge,
        fill=0.0, base=-1, channel_multiplier=-1))
    g2 = k.op("gpsimd", lambda g: g.iota(iota0.ap(), pattern=[[0, 1]], base=0,
                                         channel_multiplier=1))
    g3 = k.op("gpsimd", lambda g: g.memset(obb[:], 2.0e9))
    g4 = k.op("gpsimd", lambda g: g.memset(zt[:], 0.0))
    g6 = k.op("gpsimd", lambda g: g.memset(onesc[:], 1.0))
    g5 = k.op("gpsimd", lambda g: g.iota(ei.ap(), pattern=[[1, E]], base=0,
                                         channel_multiplier=0))
    for t in (g1, g2, g3, g4, g5, g6):
        k.wait("tensor", t)
        k.wait("vector", t)
        k.wait("sync", t)
        k.wait("scalar", t)
    lwr = k.dma("gpsimd", wr_s.ap()[:, :, 0:E], wr_d.rearrange("(a p) b -> p a b", p=128))
    lbr = k.dma("sync", br_s[:], br_d)
    k.wait("vector", lbr)
    k.op("vector", lambda v: v.tensor_copy(run[:], ei[:]))
    k.fence("vector")
    vrun = k.op("vector", lambda v: v.tensor_scalar_mul(run[:], run[:], float(C)))
    k.wait("tensor", vrun)
    nelem = E * C * 2
    flat = idxgv_d.rearrange("a b -> (a b)")
    step = 128 * 128
    for j in range((nelem + step - 1) // step):
        n0 = step * j
        n1 = min(nelem, n0 + step)
        k.dma("sync", flat[n0:n1].rearrange("(p m) -> p m", p=128),
              obb[:, 0:(n1 - n0) // 128])
    for j in range(16):
        k.dma("scalar", acc_d[128 * j:128 * (j + 1), :], zt[:])

    k.wait("tensor", lwr)
    for i in range(NT):
        k.wait("sync", k.now("tensor"))
        lxl = k.dma("sync", xl.ap(), xnf32_d[:, :, 128 * i:128 * (i + 1)]
                    .transpose([1, 0, 2]))
        k.wait("tensor", lxl)
        k.use_bank(0)
        for kc in range(8):
            k.raw("tensor", lambda t, kc=kc: t.matmul(
                k.bank[0][:, 0:E], xl[:, kc, :], wr_s[:, kc, 0:E],
                start=(kc == 0), stop=False))
        tb = k.op("tensor", lambda t: t.matmul(
            k.bank[0][:, 0:E], c["ones1_f"][0:1, 0:128], br_s[:],
            start=False, stop=True))
        k.wait("vector", tb)
        ve = k.op("vector", lambda v: v.tensor_copy(lg[:, 0:E], k.bank[0][:, 0:E]))
        k.bank_free[0] = ve
        k.fence("vector")
        if mode == "softmax_top2":
            k.op("vector", lambda v: v.reduce_max(m1[:], lg[:, 0:E], axis=AX.X,
                                                  negate=True))
            k.fence("vector")
            sx = k.op("scalar", lambda sc_: sc_.activation(gate[:, 0:E], lg[:, 0:E],
                                                           AF.Exp, bias=m1[:]))
            k.wait("vector", sx)
            k.op("vector", lambda v: v.reduce_sum(m2[:], gate[:, 0:E], axis=AX.X))
            k.fence("vector")
            k.op("vector", lambda v: v.reciprocal(m2[:], m2[:]))
            k.fence("vector")
            k.op("vector", lambda v: v.tensor_scalar(gate[:, 0:E], gate[:, 0:E], m2[:],
                                                     None, ALU.mult))
            k.fence("vector")
            k.op("vector", lambda v: v.reduce_max(m1[:], gate[:, 0:E], axis=AX.X))
            k.fence("vector")
            k.op("vector", lambda v: v.tensor_scalar(scr[:, 0:E], gate[:, 0:E], m1[:],
                                                     None, ALU.is_lt))
            k.fence("vector")
            k.op("vector", lambda v: v.tensor_tensor(scr[:, 0:E], scr[:, 0:E],
                                                     gate[:, 0:E], ALU.mult))
            k.fence("vector")
            k.op("vector", lambda v: v.reduce_max(m2[:], scr[:, 0:E], axis=AX.X))
            k.fence("vector")
            k.op("vector", lambda v: v.tensor_scalar(mask[:, 0:E], gate[:, 0:E], m2[:],
                                                     None, ALU.is_ge))
            k.fence("vector")
            vg = k.op("vector", lambda v: v.tensor_tensor(gate[:, 0:E], gate[:, 0:E],
                                                          mask[:, 0:E], ALU.mult))
        else:
            k.op("vector", lambda v: v.reduce_max(m1[:], lg[:, 0:E], axis=AX.X))
            k.fence("vector")
            k.op("vector", lambda v: v.tensor_scalar(scr[:, 0:E], lg[:, 0:E], m1[:],
                                                     None, ALU.is_ge))
            k.fence("vector")
            k.op("vector", lambda v: v.scalar_tensor_tensor(scr[:, 0:E], scr[:, 0:E],
                                                            -1e38, lg[:, 0:E], ALU.mult,
                                                            ALU.add))
            k.fence("vector")
            k.op("vector", lambda v: v.reduce_max(m2[:], scr[:, 0:E], axis=AX.X))
            k.fence("vector")
            k.op("vector", lambda v: v.tensor_scalar(mask[:, 0:E], lg[:, 0:E], m2[:],
                                                     None, ALU.is_ge))
            k.fence("vector")
            k.op("vector", lambda v: v.reduce_max(m1[:], lg[:, 0:E], axis=AX.X,
                                                  negate=True))
            k.fence("vector")
            sx = k.op("scalar", lambda sc_: sc_.activation(gate[:, 0:E], lg[:, 0:E],
                                                           AF.Exp, bias=m1[:]))
            k.wait("vector", sx)
            k.op("vector", lambda v: v.tensor_tensor(gate[:, 0:E], gate[:, 0:E],
                                                     mask[:, 0:E], ALU.mult))
            k.fence("vector")
            k.op("vector", lambda v: v.reduce_sum(m2[:], gate[:, 0:E], axis=AX.X))
            k.fence("vector")
            k.op("vector", lambda v: v.reciprocal(m2[:], m2[:]))
            k.fence("vector")
            vg = k.op("vector", lambda v: v.tensor_scalar(gate[:, 0:E], gate[:, 0:E],
                                                          m2[:], None, ALU.mult))
        k.wait("tensor", vg)
        k.use_bank(1)
        k.raw("tensor", lambda t: t.matmul(k.bank[1][:, 0:E], tris[:, :],
                                           mask.ap()[:, 0:E], start=True, stop=False))
        tp2 = k.op("tensor", lambda t: t.matmul(k.bank[1][:, 0:E],
                                                c["ones1_f"][0:1, 0:128],
                                                run[:, 0:E], start=False, stop=True))
        k.use_bank(2)
        tc = k.op("tensor", lambda t: t.matmul(k.bank[2][0:1, 0:E], onesc[:, 0:1],
                                               mask.ap()[:, 0:E], start=True, stop=True))
        k.wait("vector", tp2)
        k.op("vector", lambda v: v.tensor_scalar(scr[:, 0:E], mask[:, 0:E], -1.0, 1.0,
                                                 ALU.mult, ALU.add))
        k.fence("vector")
        vsl = k.op("vector", lambda v: v.scalar_tensor_tensor(
            scr[:, 0:E], scr[:, 0:E], 2.0e9, k.bank[1][:, 0:E], ALU.mult, ALU.add))
        k.bank_free[1] = vsl
        k.fence("vector")
        vsi = k.op("vector", lambda v: v.tensor_copy(slot_i[:, 0:E], scr[:, 0:E]))
        k.op("vector", lambda v, i=i: v.tensor_scalar_add(tokf[:], iota0[:], float(128 * i)))
        k.fence("vector")
        k.op("vector", lambda v: v.tensor_copy(
            pairs.ap()[:, :, 0:1].rearrange("p a b -> p (a b)")[:, 0:E],
            tokf.ap().broadcast_to([128, E])))
        vpr = k.op("vector", lambda v: v.tensor_copy(
            pairs.ap()[:, :, 1:2].rearrange("p a b -> p (a b)")[:, 0:E], gate[:, 0:E]))
        k.wait("vector", tc)
        vru = k.op("vector", lambda v: v.tensor_tensor(run[:, 0:E], run[:, 0:E],
                                                       k.bank[2][0:1, 0:E], ALU.add))
        k.bank_free[2] = vru
        k.wait("gpsimd", vsi)
        k.wait("gpsimd", vpr)
        for e in range(E):
            idma_scatter(idxgv_d, slot_i[:, e:e + 1], pairs.ap()[:, e, :])
    k.barrier()

    w1S = WStream(k, f"{tag}w1", w1_d[0], D, 128)
    w2S = WStream(k, f"{tag}w2", w2_d[0], F, 128)
    for e in range(E):
        lb1 = k.dma("sync", b1s.ap(), b1c_d[e])
        lb2 = k.dma("sync", b2s.ap(), b2c_d[e])
        k.wait("scalar", lb1)
        k.wait("scalar", lb2)
        gv_tok = ZERO_TOK
        for b in range(NB):
            lib = k.dma("sync", ib[:, 2 * b:2 * b + 2], idxgv_d[C * e + 128 * b:
                                                               C * e + 128 * (b + 1), :])
            k.wait("vector", lib)
            k.wait("vector", gv_tok)
            vix = k.op("vector", lambda v, b=b: v.tensor_copy(idxc[:, b:b + 1],
                                                              ib[:, 2 * b:2 * b + 1]))
            k.fence("vector")
            gtk = idma_gather(xg_tm.ap(), xntm_d, idxc[:, b:b + 1])
            k.wait("tensor", gtk)
            k.wait("tensor", c["ready"])
            for j in range(8):
                bk = 6 + (j % 2)
                k.use_bank(bk)
                tt = k.op("tensor", lambda t, j=j, bk=bk: t.transpose(
                    k.bank_bf(bk)[:, 0:128], xg_tm.ap()[:, 128 * j:128 * (j + 1)],
                    c["id_bf"][:, :]))
                k.wait("vector", tt)
                vt = k.op("vector", lambda v, j=j, b=b, bk=bk: v.tensor_copy(
                    xg_fm[:, j, 128 * b:128 * (b + 1)], k.bank_bf(bk)[:, 0:128]))
                k.bank_free[bk] = vt
                if j == 7:
                    gv_tok = vt
            k.wait("gpsimd", k.now("tensor"))
        k.wait("tensor", k.now("vector"))
        nq = (C + 511) // 512
        for f in range(FC):
            w1S.load(f, dram=w1_d[e])
            w1S.wait_tensor(f)
            for q in range(nq):
                ncol = min(512, C - 512 * q)
                bk = q % 2
                k.use_bank(bk)
                for kc in range(8):
                    tm_ = k.op("tensor", lambda t, f=f, kc=kc, bk=bk, q=q, ncol=ncol,
                               w1S=w1S:
                               t.matmul(k.bank[bk][:, 0:ncol], w1S.lhsT(f, kc),
                                        xg_fm[:, kc, 512 * q:512 * q + ncol],
                                        start=(kc == 0), stop=(kc == 7)))
                w1S.mark_used(f, tm_)
                k.wait("scalar", tm_)
                eh = k.op("scalar", lambda sc_, f=f, q=q, bk=bk, ncol=ncol: sc_.activation(
                    h_sb[:, f, 512 * q:512 * q + ncol], k.bank[bk][:, 0:ncol], act_fn,
                    bias=b1s[:, f:f + 1]))
                k.bank_free[bk] = eh
        k.wait("tensor", k.now("scalar"))
        for o in range(8):
            w2S.load(o, dram=w2_d[e])
            w2S.wait_tensor(o)
            for q in range(nq):
                ncol = min(512, C - 512 * q)
                bk = 2 + q % 2
                k.use_bank(bk)
                for kc in range(FC):
                    ty = k.op("tensor", lambda t, o=o, kc=kc, bk=bk, q=q, ncol=ncol,
                              w2S=w2S:
                              t.matmul(k.bank[bk][:, 0:ncol], w2S.lhsT(o, kc),
                                       h_sb[:, kc, 512 * q:512 * q + ncol],
                                       start=(kc == 0), stop=(kc == FC - 1)))
                w2S.mark_used(o, ty)
                k.wait("scalar", ty)
                ey = k.op("scalar", lambda sc_, o=o, q=q, bk=bk, ncol=ncol: sc_.activation(
                    y_fm[:, o, 512 * q:512 * q + ncol], k.bank[bk][:, 0:ncol],
                    AF.Identity, bias=b2s[:, o:o + 1]))
                k.bank_free[bk] = ey
        k.wait("tensor", k.now("scalar"))
        prev_sc = ZERO_TOK
        for b in range(NB):
            k.wait("tensor", prev_sc)
            for j in range(8):
                bk = 6 + (j % 2)
                k.use_bank(bk)
                tt = k.op("tensor", lambda t, j=j, b=b, bk=bk: t.transpose(
                    k.bank[bk][:, 0:128], y_fm.ap()[:, j, 128 * b:128 * (b + 1)],
                    c["id_f"][:, :]))
                k.wait("vector", tt)
                vt = k.op("vector", lambda v, j=j, bk=bk: v.tensor_copy(
                    y_tm[:, 128 * j:128 * (j + 1)], k.bank[bk][:, 0:128]))
                k.bank_free[bk] = vt
            vsc = k.op("vector", lambda v, b=b: v.tensor_scalar(
                y_tm[:, :], y_tm[:, :], ib[:, 2 * b + 1:2 * b + 2], None, ALU.mult))
            k.wait("gpsimd", vsc)
            prev_sc = idma_scatter_add(acc_d, idxc[:, b:b + 1], y_tm.ap())
            k.wait("vector", prev_sc)
    k.barrier()
    return k.now("gpsimd")



def emit_l2_attn(k, c, xnfm_d, pqkv_d, negmb_col_d, pwo_d, yattn_d):
    G, HPG = 4, 4
    xn_st = k.sb("a2_xn", [128, 8, 1024], BF16)
    q_om = k.sb("a2_qom", [128, 2, HALF], BF16)
    q_hm = k.sb("a2_qhm", [64, HPG, HALF], BF16)
    ktmB = k.sb("a2_ktm", [64, 64, HPG, 64], BF16)
    vaug = k.sb("a2_vaug", [64, 64, HPG, 65], BF16)
    kvf = k.sb("a2_kvf", [128, 2, 1024], BF16)
    stg_tm = k.sb("a2_stm", [128, 256], BF16)
    qe = k.sb("a2_qe", [128, 256], F32)
    qsum = k.sb("a2_qsum", [128, 4], F32)
    nmb = k.sb("a2_nmb", [128, 32], F32)
    ctx_sb = k.sb("a2_ctx", [64, HPG, 65], BF16)
    ctx_f = k.sb("a2_ctxf", [64, HPG, 65], F32)
    den = k.sb("a2_den", [1, HPG, HALF], F32)
    rec = den
    recb = k.sb("a2_recb", [128, 512], F32)
    xo_fm = k.sb("a2_xo", [128, 8, HALF], BF16)
    qkvS = WStream(k, "a2_qkv", pqkv_d, D, 128, nbuf=3)

    lnm = k.dma("sync", nmb[:], negmb_col_d)
    k.wait("scalar", lnm)
    prev_vk2 = ZERO_TOK

    for g in range(G):
        if g > 0:
            k.wait("sync", k.now("tensor"))
            k.wait("sync", k.now("vector"))
        for sc in range(4):
            t0 = 1024 * sc
            if sc > 0 or g > 0:
                k.wait("sync", k.now("tensor"))
                k.wait("sync", k.now("vector"))
            lx = k.dma("sync", xn_st.ap(), xnfm_d[:, :, t0:t0 + 1024].transpose([1, 0, 2]))
            k.wait("tensor", lx)
            blocks = [2, 3, 4, 5] if t0 < HALF else [0, 1, 2, 3, 4, 5]
            for f in blocks:
                kind = (0 if f < 2 else (1 if f < 4 else 2))
                fb = 6 * g + f
                fi = f % 2
                qkvS.load(fb)
                qkvS.wait_tensor(fb)
                for q2 in range(2):
                    bk = q2
                    k.use_bank(bk)
                    for kc in range(8):
                        tm_ = k.op("tensor", lambda t, fb=fb, kc=kc, bk=bk, q2=q2:
                                   t.matmul(k.bank[bk][:, :], qkvS.lhsT(fb, kc),
                                            xn_st[:, kc, 512 * q2:512 * (q2 + 1)],
                                            start=(kc == 0), stop=(kc == 7)))
                    qkvS.mark_used(fb, tm_)
                    k.wait("vector", tm_)
                    if kind == 0:
                        qcl = t0 + 512 * q2 - HALF
                        ev = k.op("vector", lambda v, fi=fi, qcl=qcl, bk=bk: v.tensor_copy(
                            q_om[:, fi, qcl:qcl + 512], k.bank[bk][:, :]))
                    else:
                        ev = k.op("vector", lambda v, fi=fi, q2=q2, bk=bk: v.tensor_copy(
                            kvf[:, fi, 512 * q2:512 * (q2 + 1)], k.bank[bk][:, :]))
                    k.bank_free[bk] = ev
                if kind == 0:
                    continue
                k.wait("tensor", k.now("vector"))
                for tt8 in range(8):
                    tti = 8 * sc + tt8
                    bk = 6 + (tt8 % 2)
                    k.use_bank(bk)
                    ttk = k.op("tensor", lambda t, fi=fi, tt8=tt8, bk=bk: t.transpose(
                        k.bank_bf(bk)[:, 0:128], kvf.ap()[:, fi, 128 * tt8:128 * (tt8 + 1)],
                        c["id_bf"][:, :]))
                    if kind == 1:
                        k.wait("scalar", ttk)
                        k.wait("scalar", prev_vk2)
                        se = k.op("scalar", lambda sc_, bk=bk, tti=tti: sc_.activation(
                            stg_tm[:, 0:128], k.bank_bf(bk)[:, 0:128], AF.Exp,
                            bias=nmb[:, tti:tti + 1]))
                        k.bank_free[bk] = se
                        k.wait("vector", se)
                        k.op("vector", lambda v, fi=fi, tti=tti: v.tensor_copy(
                            ktmB.ap()[0:64, 2 * tti, 2 * fi:2 * fi + 2, :],
                            stg_tm.ap()[0:64, 0:128].rearrange("p (a b) -> p a b", b=64)))
                        vk2 = k.op("vector", lambda v, fi=fi, tti=tti: v.tensor_copy(
                            ktmB.ap()[0:64, 2 * tti + 1, 2 * fi:2 * fi + 2, :],
                            stg_tm.ap()[64:128, 0:128].rearrange("p (a b) -> p a b", b=64)))
                        prev_vk2 = vk2
                    else:
                        k.wait("vector", ttk)
                        k.op("vector", lambda v, fi=fi, tti=tti, bk=bk: v.tensor_copy(
                            vaug.ap()[0:64, 2 * tti, 2 * fi:2 * fi + 2, 0:64],
                            k.bank_bf(bk)[0:64, 0:128].rearrange("p (a b) -> p a b", b=64)))
                        vk2 = k.op("vector", lambda v, fi=fi, tti=tti, bk=bk: v.tensor_copy(
                            vaug.ap()[0:64, 2 * tti + 1, 2 * fi:2 * fi + 2, 0:64],
                            k.bank_bf(bk)[64:128, 0:128].rearrange("p (a b) -> p a b", b=64)))
                        k.bank_free[bk] = vk2
            zon = k.op("gpsimd", lambda g_, sc=sc: g_.memset(
                vaug.ap()[0:64, 16 * sc:16 * (sc + 1), :, 64:65]
                .rearrange("p a c b -> p (a c b)"), 1.0))
            k.wait("tensor", zon)
            k.wait("vector", zon)
        for tti in range(16):
            for fi in range(2):
                bk = 6 + (fi % 2)
                k.use_bank(bk)
                tq = k.op("tensor", lambda t, fi=fi, tti=tti, bk=bk: t.transpose(
                    k.bank_bf(bk)[:, 0:128], q_om.ap()[:, fi, 128 * tti:128 * (tti + 1)],
                    c["id_bf"][:, :]))
                k.wait("vector", tq)
                vq = k.op("vector", lambda v, fi=fi, bk=bk: v.tensor_copy(
                    stg_tm[:, 128 * fi:128 * (fi + 1)], k.bank_bf(bk)[:, 0:128]))
                k.bank_free[bk] = vq
            k.wait("scalar", k.now("vector"))
            sE = k.op("scalar", lambda sc_: sc_.activation(qe[:], stg_tm[:], AF.Exp))
            k.wait("vector", sE)
            k.op("vector", lambda v: v.reduce_sum(
                qsum.ap(), qe.ap().rearrange("p (a b) -> p a b", b=64), axis=AX.X))
            k.fence("vector")
            k.op("vector", lambda v: v.tensor_scalar_mul(qsum.ap(), qsum.ap(), 8.0))
            k.fence("vector")
            k.op("vector", lambda v: v.reciprocal(qsum.ap(), qsum.ap()))
            k.fence("vector")
            vqs = k.op("vector", lambda v: v.tensor_tensor(
                stg_tm.ap().rearrange("p (a b) -> p a b", b=64),
                qe.ap().rearrange("p (a b) -> p a b", b=64),
                qsum.ap().unsqueeze(2).broadcast_to([128, 4, 64]), ALU.mult))
            k.wait("tensor", vqs)
            for fi in range(2):
                bk = 6 + (fi % 2)
                k.use_bank(bk)
                tq2 = k.op("tensor", lambda t, fi=fi, bk=bk: t.transpose(
                    k.bank_bf(bk)[:, 0:128], stg_tm.ap()[:, 128 * fi:128 * (fi + 1)],
                    c["id_bf"][:, :]))
                k.wait("vector", tq2)
                k.op("vector", lambda v, fi=fi, tti=tti, bk=bk: v.tensor_copy(
                    q_hm[0:64, 2 * fi, 128 * tti:128 * (tti + 1)],
                    k.bank_bf(bk)[0:64, 0:128]))
                vq3 = k.op("vector", lambda v, fi=fi, tti=tti, bk=bk: v.tensor_copy(
                    q_hm[0:64, 2 * fi + 1, 128 * tti:128 * (tti + 1)],
                    k.bank_bf(bk)[64:128, 0:128]))
                k.bank_free[bk] = vq3
        k.wait("tensor", k.now("vector"))
        zc = k.op("gpsimd", lambda g_: g_.memset(
            ctx_f.ap().rearrange("p a b -> p (a b)"), 0.0))
        k.wait("vector", zc)
        k.use_bank(6)
        k.use_bank(7)
        for n in range(64):
            if n >= 32:
                vc = k.op("vector", lambda v: v.tensor_copy(
                    ctx_sb.ap().rearrange("p a b -> p (a b)"),
                    ctx_f.ap().rearrange("p a b -> p (a b)")))
                k.wait("tensor", vc)
                qb = n - 32
                for hh in range(HPG):
                    ta = k.op("tensor", lambda t, hh=hh, qb=qb: t.matmul(
                        k.bank[7][0:65, 64 * hh:64 * (hh + 1)],
                        ctx_sb[0:64, hh, :], q_hm[0:64, hh, 64 * qb:64 * (qb + 1)],
                        start=True, stop=True))
                k.wait("vector", k.now("tensor"))
                vd = k.op("vector", lambda v, qb=qb: v.tensor_copy(
                    den.ap()[0:1, :, 64 * qb:64 * (qb + 1)],
                    k.bank[7].ap()[64:65, 0:256].rearrange("p (a b) -> p a b", b=64)))
                for hh in range(HPG):
                    h = HPG * g + hh
                    po = 64 * (h % 2)
                    vx = k.op("vector", lambda v, hh=hh, h=h, po=po, qb=qb: v.tensor_copy(
                        xo_fm[po:po + 64, h // 2, 64 * qb:64 * (qb + 1)],
                        k.bank[7].ap()[0:64, 64 * hh:64 * (hh + 1)]))
                k.bank_free[7] = k.now("vector")
                k.use_bank(7)
            k.use_bank(6)
            for hh in range(HPG):
                tacc = k.op("tensor", lambda t, n=n, hh=hh: t.matmul(
                    k.bank[6][0:64, 65 * hh:65 * (hh + 1)],
                    ktmB[0:64, n, hh, :], vaug[0:64, n, hh, :],
                    start=(hh == 0), stop=(hh == HPG - 1)))
            k.wait("vector", tacc)
            vadd = k.op("vector", lambda v: v.tensor_tensor(
                ctx_f.ap().rearrange("p a b -> p (a b)"),
                ctx_f.ap().rearrange("p a b -> p (a b)"),
                k.bank[6].ap()[0:64, 0:260], ALU.add))
            k.bank_free[6] = vadd
        k.wait("vector", k.now("tensor"))
        k.fence("vector")
        k.op("vector", lambda v: v.tensor_scalar_max(
            rec.ap().rearrange("p a b -> p (a b)"),
            den.ap().rearrange("p a b -> p (a b)"), 1e-6))
        k.fence("vector")
        vr = k.op("vector", lambda v: v.reciprocal(
            rec.ap().rearrange("p a b -> p (a b)"), rec.ap().rearrange("p a b -> p (a b)")))
        k.wait("tensor", vr)
        for hh in range(HPG):
            h = HPG * g + hh
            po = 64 * (h % 2)
            for q4 in range(4):
                k.use_bank(5)
                tR = k.op("tensor", lambda t, hh=hh, q4=q4: t.matmul(
                    k.bank[5][0:64, 0:512], c["ones1_f"][0:1, 0:64],
                    rec.ap()[0:1, hh, 512 * q4:512 * (q4 + 1)], start=True, stop=True))
                k.wait("vector", tR)
                vRb = k.op("vector", lambda v, po=po: v.tensor_copy(
                    recb[po:po + 64, :], k.bank[5][0:64, 0:512]))
                k.bank_free[5] = vRb
                k.fence("vector")
                vO = k.op("vector", lambda v, h=h, po=po, q4=q4: v.tensor_tensor(
                    xo_fm[po:po + 64, h // 2, 512 * q4:512 * (q4 + 1)],
                    xo_fm[po:po + 64, h // 2, 512 * q4:512 * (q4 + 1)],
                    recb[po:po + 64, :], ALU.mult))
        k.bank_free[6] = k.now("vector")
    if DEBUG_DUMP:
        dxo = k.dtmp("t_dbgxo2", [128, 8, HALF], BF16)
        dden = k.dtmp("t_dbgden2", [1, HPG, HALF], F32)
        dktm = k.dtmp("t_dbgktm", [64, 64, HPG, 64], BF16)
        dqhm = k.dtmp("t_dbgqhm", [64, HPG, HALF], BF16)
        k.barrier()
        k.dma("sync", dxo, xo_fm.ap())
        k.dma("sync", dden, den.ap())
        k.dma("sync", dktm, ktmB.ap())
        k.dma("sync", dqhm, q_hm.ap())
        k.barrier()
    woS = WStream(k, "a2_wo", pwo_d, D, 128)
    k.wait("tensor", k.now("vector"))
    yst2 = k.sb("a2_yst", [128, 512], F32)
    prev_st = ZERO_TOK
    for o in range(8):
        woS.load(o)
        woS.wait_tensor(o)
        for q in range(4):
            bk = q % 2
            k.use_bank(bk)
            for kc in range(8):
                tw = k.op("tensor", lambda t, o=o, kc=kc, bk=bk, q=q: t.matmul(
                    k.bank[bk][:, :], woS.lhsT(o, kc),
                    xo_fm[:, kc, 512 * q:512 * (q + 1)],
                    start=(kc == 0), stop=(kc == 7)))
            woS.mark_used(o, tw)
            k.wait("scalar", tw)
            k.wait("scalar", prev_st)
            eW = k.op("scalar", lambda sc_, bk=bk: sc_.activation(
                yst2[:], k.bank[bk][:, :], AF.Copy))
            k.bank_free[bk] = eW
            k.wait("sync", eW)
            prev_st = k.dma("sync", yattn_d[128 * o:128 * (o + 1), 512 * q:512 * (q + 1)],
                            yst2[:])
    return k.now("sync")


def build_l2():
    k = KB("l2")
    E, C = 8, 768
    x_d = k.din("x", [S, D], F32)
    negmb_d = k.din("negmb", [128, 32], F32)
    pqkv_d = k.din("pqkv", [D, 3 * D], F32)
    pwo_d = k.din("pwo", [D, D], F32)
    wr_d = k.din("wr", [D, E], F32)
    br_d = k.din("br", [1, E], F32)
    w1_d = k.din("w1", [E, D, 4096], F32)
    b1c_d = k.din("b1c", [E, 128, 32], F32)
    w2_d = k.din("w2", [E, 4096, D], F32)
    b2c_d = k.din("b2c", [E, 128, 8], F32)
    y_d = k.dout("y", [HALF, D], F32)

    xnfm_d = k.dtmp("t_xnfm", [8, 128, S], BF16)
    xnf32_d = k.dtmp("t_xnf32", [8, 128, HALF], F32)
    yattn_d = k.dtmp("t_yattn", [D, HALF], F32)
    yffn_d = k.dtmp("t_yffn", [D, HALF], F32)

    c = emit_consts(k)
    k.open_scope()
    xn_fm = k.sb("xn_fm", [128, 8, S], BF16)
    emit_ln_xnfm(k, c, x_d, xn_fm, nt=S // 128, f32fm_d=xnf32_d)
    k.barrier()
    k.dma("sync", xnfm_d.transpose([1, 0, 2]), xn_fm.ap())
    k.close_scope()
    k.open_scope()
    emit_moe_dense(k, c, "m2", xnfm_d, xnf32_d, yffn_d, E, C,
                   "top2_softmax", AF.Gelu, wr_d, br_d, w1_d, b1c_d, w2_d, b2c_d)
    k.close_scope()
    k.open_scope()
    emit_l2_attn(k, c, xnfm_d, pqkv_d, negmb_d, pwo_d, yattn_d)
    k.close_scope()
    k.open_scope()
    emit_residual_out(k, c, x_d, [(yattn_d, "fm"), (yffn_d, "fm")], y_d)
    k.close_scope()
    return k.finish()


def _prep_l2(inputs, core):
    x_eff, m = _prep_common_x(inputs, core)
    A = np.ascontiguousarray
    P = np.asarray(inputs["L2_attn_W_qkv"]).T
    blocks = []
    for g in range(4):
        blocks += [P[:, 256 * g:256 * (g + 1)],
                   P[:, 1024 + 256 * g:1024 + 256 * (g + 1)],
                   P[:, 2048 + 256 * g:2048 + 256 * (g + 1)]]
    pqkv = A(np.concatenate(blocks, axis=1), dtype=np.float32)
    b1 = np.asarray(inputs["L2_ffn_b1"])
    b2 = np.asarray(inputs["L2_ffn_b2"])
    return dict(
        x=x_eff,
        negmb=A((-BIG * (1.0 - m)).reshape(32, 128).T, dtype=np.float32),
        pqkv=pqkv,
        pwo=A(np.asarray(inputs["L2_attn_W_out"]).T, dtype=np.float32),
        wr=A(np.asarray(inputs["L2_ffn_gate_w"]), dtype=np.float32),
        br=A(np.asarray(inputs["L2_ffn_gate_b"]).reshape(1, 8), dtype=np.float32),
        w1=A(np.asarray(inputs["L2_ffn_w1"]), dtype=np.float32),
        b1c=A(b1.reshape(8, 32, 128).transpose(0, 2, 1), dtype=np.float32),
        w2=A(np.asarray(inputs["L2_ffn_w2"]), dtype=np.float32),
        b2c=A(b2.reshape(8, 8, 128).transpose(0, 2, 1), dtype=np.float32),
    )


_BUILDERS[2] = (build_l2, _prep_l2)



def emit_moe_dense(k, c, tag, xnfm_d, xnf32_d, yffn_d, E, C_unused, mode, act_fn,
                   wr_d, br_d, w1_d, b1c_d, w2_d, b2c_d, F=4096):
    FC = F // 128
    xl = k.sb(f"{tag}_xl", [128, 8, 128], F32)
    wr_s = k.sb(f"{tag}_wr", [128, 8, E], F32)
    br_s = k.sb(f"{tag}_br", [1, E], F32)
    lg = k.sb(f"{tag}_lg", [128, E], F32)
    gate = k.sb(f"{tag}_gate", [128, E], F32)
    mask = k.sb(f"{tag}_mask", [128, E], F32)
    m1 = k.sb(f"{tag}_m1", [128, 1], F32)
    m2 = k.sb(f"{tag}_m2", [128, 1], F32)
    scr = k.sb(f"{tag}_scr", [128, E], F32)
    gcol = k.sb(f"{tag}_gcol", [16, HALF], F32)
    xn_st = k.sb(f"{tag}_xn", [128, 8, 1024], BF16)
    h_sb = k.sb(f"{tag}_h", [128, FC, 1024], BF16)
    b1s = k.sb(f"{tag}_b1s", [128, FC], F32)
    b2s = k.sb(f"{tag}_b2s", [128, 8], F32)
    acc = k.sb(f"{tag}_acc", [128, 8, 1024], F32)
    yst = k.sb(f"{tag}_yst", [128, 512], F32)
    gbs = k.sb(f"{tag}_gbs", [128, 512], F32)

    lwr = k.dma("sync", wr_s.ap()[:, :, 0:E], wr_d.rearrange("(a p) b -> p a b", p=128))
    lbr = k.dma("sync", br_s[:], br_d)
    k.wait("vector", lbr)
    k.wait("tensor", lwr)

    prev_tg = ZERO_TOK
    for i in range(16):
        k.wait("sync", k.now("tensor"))
        lxl = k.dma("sync", xl.ap(), xnf32_d[:, :, 128 * i:128 * (i + 1)]
                    .transpose([1, 0, 2]))
        k.wait("tensor", lxl)
        k.use_bank(0)
        for kc in range(8):
            k.raw("tensor", lambda t, kc=kc: t.matmul(
                k.bank[0][:, 0:E], xl[:, kc, :], wr_s[:, kc, 0:E],
                start=(kc == 0), stop=False))
        tb = k.op("tensor", lambda t: t.matmul(
            k.bank[0][:, 0:E], c["ones1_f"][0:1, 0:128], br_s[:],
            start=False, stop=True))
        k.wait("vector", tb)
        ve = k.op("vector", lambda v: v.tensor_copy(lg[:, 0:E], k.bank[0][:, 0:E]))
        k.bank_free[0] = ve
        k.fence("vector")
        if mode == "softmax_top2":
            k.op("vector", lambda v: v.reduce_max(m1[:], lg[:, 0:E], axis=AX.X,
                                                  negate=True))
            k.fence("vector")
            k.wait("scalar", k.now("vector"))
            k.wait("scalar", prev_tg)
            sx = k.op("scalar", lambda sc_: sc_.activation(gate[:, 0:E], lg[:, 0:E],
                                                           AF.Exp, bias=m1[:]))
            k.wait("vector", sx)
            k.op("vector", lambda v: v.reduce_sum(m2[:], gate[:, 0:E], axis=AX.X))
            k.fence("vector")
            k.op("vector", lambda v: v.reciprocal(m2[:], m2[:]))
            k.fence("vector")
            k.op("vector", lambda v: v.tensor_scalar(gate[:, 0:E], gate[:, 0:E], m2[:],
                                                     None, ALU.mult))
            k.fence("vector")
            k.op("vector", lambda v: v.reduce_max(m1[:], gate[:, 0:E], axis=AX.X))
            k.fence("vector")
            k.op("vector", lambda v: v.tensor_scalar(scr[:, 0:E], gate[:, 0:E], m1[:],
                                                     None, ALU.is_lt))
            k.fence("vector")
            k.op("vector", lambda v: v.tensor_tensor(scr[:, 0:E], scr[:, 0:E],
                                                     gate[:, 0:E], ALU.mult))
            k.fence("vector")
            k.op("vector", lambda v: v.reduce_max(m2[:], scr[:, 0:E], axis=AX.X))
            k.fence("vector")
            k.op("vector", lambda v: v.tensor_scalar(mask[:, 0:E], gate[:, 0:E], m2[:],
                                                     None, ALU.is_ge))
            k.fence("vector")
            vg = k.op("vector", lambda v: v.tensor_tensor(gate[:, 0:E], gate[:, 0:E],
                                                          mask[:, 0:E], ALU.mult))
        else:
            k.op("vector", lambda v: v.reduce_max(m1[:], lg[:, 0:E], axis=AX.X))
            k.fence("vector")
            k.op("vector", lambda v: v.tensor_scalar(scr[:, 0:E], lg[:, 0:E], m1[:],
                                                     None, ALU.is_ge))
            k.fence("vector")
            k.op("vector", lambda v: v.scalar_tensor_tensor(scr[:, 0:E], scr[:, 0:E],
                                                            -1e38, lg[:, 0:E], ALU.mult,
                                                            ALU.add))
            k.fence("vector")
            k.op("vector", lambda v: v.reduce_max(m2[:], scr[:, 0:E], axis=AX.X))
            k.fence("vector")
            k.op("vector", lambda v: v.tensor_scalar(mask[:, 0:E], lg[:, 0:E], m2[:],
                                                     None, ALU.is_ge))
            k.fence("vector")
            k.op("vector", lambda v: v.reduce_max(m1[:], lg[:, 0:E], axis=AX.X,
                                                  negate=True))
            k.fence("vector")
            k.wait("scalar", k.now("vector"))
            k.wait("scalar", prev_tg)
            sx = k.op("scalar", lambda sc_: sc_.activation(gate[:, 0:E], lg[:, 0:E],
                                                           AF.Exp, bias=m1[:]))
            k.wait("vector", sx)
            k.op("vector", lambda v: v.tensor_tensor(gate[:, 0:E], gate[:, 0:E],
                                                     mask[:, 0:E], ALU.mult))
            k.fence("vector")
            k.op("vector", lambda v: v.reduce_sum(m2[:], gate[:, 0:E], axis=AX.X))
            k.fence("vector")
            k.op("vector", lambda v: v.reciprocal(m2[:], m2[:]))
            k.fence("vector")
            vg = k.op("vector", lambda v: v.tensor_scalar(gate[:, 0:E], gate[:, 0:E],
                                                          m2[:], None, ALU.mult))
        k.wait("tensor", vg)
        bk = 6 + (i % 2)
        k.use_bank(bk)
        tg = k.op("tensor", lambda t, bk=bk: t.transpose(
            k.bank[bk][0:E, 0:128], gate[:, 0:E], c["id_f"][:, :]))
        prev_tg = tg
        k.wait("vector", tg)
        vgc = k.op("vector", lambda v, i=i, bk=bk: v.tensor_copy(
            gcol[0:E, 128 * i:128 * (i + 1)], k.bank[bk][0:E, 0:128]))
        k.bank_free[bk] = vgc

    w1S = WStream(k, f"{tag}w1", w1_d[0], D, 128)
    w2S = WStream(k, f"{tag}w2", w2_d[0], F, 128)
    prev_va2 = ZERO_TOK
    for tb2 in range(2):
        t0 = HALF + 1024 * tb2
        k.wait("sync", k.now("tensor"))
        lxs = k.dma("sync", xn_st.ap(), xnfm_d[:, :, t0:t0 + 1024].transpose([1, 0, 2]))
        k.wait("tensor", lxs)
        k.wait("gpsimd", (k.q["sync"], k.q["sync"].n))
        zacc = k.op("gpsimd", lambda g_: g_.memset(
            acc.ap().rearrange("p a b -> p (a b)"), 0.0))
        k.wait("vector", zacc)
        k.wait("tensor", zacc)
        for e in range(E):
            lb1 = k.dma("sync", b1s.ap(), b1c_d[e])
            lb2 = k.dma("sync", b2s.ap(), b2c_d[e])
            k.wait("scalar", lb1)
            k.wait("scalar", lb2)
            for f in range(FC):
                w1S.load(f, dram=w1_d[e])
                w1S.wait_tensor(f)
                for q in range(2):
                    bk = q
                    k.use_bank(bk)
                    for kc in range(8):
                        tm_ = k.op("tensor", lambda t, f=f, kc=kc, bk=bk, q=q, w1S=w1S:
                                   t.matmul(k.bank[bk][:, :], w1S.lhsT(f, kc),
                                            xn_st[:, kc, 512 * q:512 * (q + 1)],
                                            start=(kc == 0), stop=(kc == 7)))
                    w1S.mark_used(f, tm_)
                    k.wait("scalar", tm_)
                    eh = k.op("scalar", lambda sc_, f=f, q=q, bk=bk: sc_.activation(
                        h_sb[:, f, 512 * q:512 * (q + 1)], k.bank[bk][:, :], act_fn,
                        bias=b1s[:, f:f + 1]))
                    k.bank_free[bk] = eh
            k.wait("tensor", k.now("scalar"))
            for o in range(8):
                w2S.load(o, dram=w2_d[e])
                w2S.wait_tensor(o)
                for q in range(2):
                    bk = 2 + q
                    k.use_bank(bk)
                    for kc in range(FC):
                        ty = k.op("tensor", lambda t, o=o, kc=kc, bk=bk, q=q, w2S=w2S:
                                  t.matmul(k.bank[bk][:, :], w2S.lhsT(o, kc),
                                           h_sb[:, kc, 512 * q:512 * (q + 1)],
                                           start=(kc == 0), stop=(kc == FC - 1)))
                    w2S.mark_used(o, ty)
                    bq = 4 + q
                    k.use_bank(bq)
                    tgb = k.op("tensor", lambda t, e=e, q=q, tb2=tb2, bq=bq: t.matmul(
                        k.bank[bq][:, :], c["E16"][:, 128 * e:128 * (e + 1)],
                        gcol[0:16, 1024 * tb2 + 512 * q:1024 * tb2 + 512 * (q + 1)],
                        start=True, stop=True))
                    k.wait("scalar", ty)
                    k.wait("scalar", prev_va2)
                    ey = k.op("scalar", lambda sc_, o=o, bk=bk: sc_.activation(
                        yst[:], k.bank[bk][:, :], AF.Identity, bias=b2s[:, o:o + 1]))
                    k.bank_free[bk] = ey
                    k.wait("vector", ey)
                    k.wait("vector", tgb)
                    vac = k.op("vector", lambda v, o=o, q=q, bq=bq: v.scalar_tensor_tensor(
                        yst[:], yst[:], 1.0, k.bank[bq][:, :], ALU.mult, ALU.mult))
                    k.bank_free[bq] = vac
                    k.fence("vector")
                    va2 = k.op("vector", lambda v, o=o, q=q: v.tensor_tensor(
                        acc[:, o, 512 * q:512 * (q + 1)], acc[:, o, 512 * q:512 * (q + 1)],
                        yst[:], ALU.add))
                    prev_va2 = va2
        k.wait("sync", k.now("vector"))
        for o in range(8):
            k.dma("sync", yffn_d[128 * o:128 * (o + 1), 1024 * tb2:1024 * (tb2 + 1)],
                  acc[:, o, :])
    k.barrier()
    return k.now("sync")



def emit_l1_attn(k, c, xnfm_d, pin_d, conv_d, pxp_d, alog_d, dvec_d, pout_d, yattn_d):
    NP = 4
    PT = 512
    xn_st = k.sb("a1_xn", [128, 8, PT + 3], BF16)
    xh = k.sb("a1_xh", [128, 16, PT + 3], BF16)
    zf = k.sb("a1_z", [128, 16, PT], BF16)
    xcf = k.sb("a1_xc", [128, 16, PT], BF16)
    xct = k.sb("a1_xct", [128, 4, 2048], BF16)
    y2t = k.sb("a1_y2t", [128, 4, 2048], BF16)
    y2f = k.sb("a1_y2f", [128, 16, PT], BF16)
    wrk = k.sb("a1_wrk", [128, 2048], F32)
    aexp = k.sb("a1_aexp", [128, 2048], F32)
    asl = k.sb("a1_asl", [1, 2048], F32)
    dsl = k.sb("a1_dsl", [1, 2048], F32)
    db = k.sb("a1_db", [128, 128], F32)
    convs = k.sb("a1_conv", [128, 16, 4], F32)
    pxp = k.sb("a1_pxp", [128, 16, 33], BF16)
    dcb33 = k.sb("a1_dcb", [33, PT], F32)
    dcb_tm = k.sb("a1_dcbt", [128, 4, 33], F32)
    delta = k.sb("a1_del", [128, 4], F32)
    bc_tm = k.sb("a1_bc", [128, 4, 16], F32)
    sps = k.sb("a1_sps", [128, 4], F32)
    y1 = k.sb("a1_y1", [128, 128], F32)
    yst = k.sb("a1_yst", [128, 512], F32)
    pinS = WStream(k, "a1_pin", pin_d, D, 128, nbuf=3)
    poutS = WStream(k, "a1_pout", pout_d, 2048, 128, nbuf=2)

    lcv = k.dma("sync", convs.ap(), conv_d)
    lpx = k.dma("gpsimd", pxp.ap(), pxp_d.rearrange("(a p) b -> p a b", p=128))
    ldv = k.dma("sync", dsl[:], dvec_d)
    k.wait("vector", lcv)
    k.wait("tensor", lpx)

    for p in range(NP):
        t0 = HALF + PT * p
        k.wait("sync", k.now("tensor"))
        k.wait("sync", k.now("vector"))
        lx = k.dma("sync", xn_st.ap(),
                   xnfm_d[:, :, t0 - 3:t0 + PT].transpose([1, 0, 2]))
        k.wait("tensor", lx)
        for f in range(32):
            pinS.load(f)
            pinS.wait_tensor(f)
            is_xh = f < 16
            bk = f % 2
            k.use_bank(bk)
            k.use_bank(bk + 2)
            if is_xh:
                for c0, nc_ in ((0, 512), (512, 3)):
                    for kc in range(8):
                        tm_ = k.op("tensor", lambda t, f=f, kc=kc, bk=bk, c0=c0, nc_=nc_:
                                   t.matmul(k.bank[bk][:, 0:nc_] if c0 == 0 else
                                            k.bank[bk + 2][:, 0:nc_],
                                            pinS.lhsT(f, kc),
                                            xn_st[:, kc, c0:c0 + nc_],
                                            start=(kc == 0), stop=(kc == 7)))
            else:
                for kc in range(8):
                    tm_ = k.op("tensor", lambda t, f=f, kc=kc, bk=bk:
                               t.matmul(k.bank[bk][:, 0:512], pinS.lhsT(f, kc),
                                        xn_st[:, kc, 3:PT + 3],
                                        start=(kc == 0), stop=(kc == 7)))
            pinS.mark_used(f, tm_)
            k.wait("vector", tm_)
            fi = f % 16
            if is_xh:
                k.op("vector", lambda v, fi=fi, bk=bk: v.tensor_copy(
                    xh[:, fi, 0:512], k.bank[bk][:, 0:512]))
                ev = k.op("vector", lambda v, fi=fi, bk=bk: v.tensor_copy(
                    xh[:, fi, 512:PT + 3], k.bank[bk + 2][:, 0:3]))
            else:
                ev = k.op("vector", lambda v, fi=fi, bk=bk: v.tensor_copy(
                    zf[:, fi, :], k.bank[bk][:, 0:512]))
            k.bank_free[bk] = ev
            k.bank_free[bk + 2] = ev
        k.wait("vector", k.now("tensor"))
        for ch in range(16):
            k.op("vector", lambda v, ch=ch: v.tensor_scalar(
                wrk[:, 0:PT], xh[:, ch, 0:PT], convs[:, ch, 0:1], None, ALU.mult))
            k.fence("vector")
            for kk in range(1, 4):
                k.op("vector", lambda v, ch=ch, kk=kk: v.scalar_tensor_tensor(
                    wrk[:, 0:PT], xh[:, ch, kk:kk + PT], convs[:, ch, kk:kk + 1],
                    wrk[:, 0:PT], ALU.mult, ALU.add))
                k.fence("vector")
            k.wait("scalar", k.now("vector"))
            sv = k.op("scalar", lambda sc_, ch=ch: sc_.activation(
                xcf[:, ch, :], wrk[:, 0:PT], AF.Silu))
            k.wait("vector", sv)
        k.wait("tensor", k.now("scalar"))
        for tt4 in range(4):
            for ch in range(16):
                bk = 6 + (ch % 2)
                k.use_bank(bk)
                ttp = k.op("tensor", lambda t, tt4=tt4, ch=ch, bk=bk: t.transpose(
                    k.bank_bf(bk)[:, 0:128], xcf.ap()[:, ch, 128 * tt4:128 * (tt4 + 1)],
                    c["id_bf"][:, :]))
                k.wait("vector", ttp)
                vtp = k.op("vector", lambda v, tt4=tt4, ch=ch, bk=bk: v.tensor_copy(
                    xct[:, tt4, 128 * ch:128 * (ch + 1)], k.bank_bf(bk)[:, 0:128]))
                k.bank_free[bk] = vtp
        k.use_bank(5)
        for kc in range(16):
            tp3 = k.op("tensor", lambda t, kc=kc: t.matmul(
                k.bank[5][0:33, 0:PT], pxp[:, kc, :], xcf[:, kc, :],
                start=(kc == 0), stop=(kc == 15)))
        k.wait("vector", tp3)
        vp3 = k.op("vector", lambda v: v.tensor_copy(dcb33[:, 0:PT], k.bank[5][0:33, 0:PT]))
        k.bank_free[5] = vp3
        k.wait("tensor", vp3)
        for tt4 in range(4):
            bk = 6 + (tt4 % 2)
            k.use_bank(bk)
            tt3 = k.op("tensor", lambda t, tt4=tt4, bk=bk: t.transpose(
                k.bank[bk][0:33, 0:128] if False else k.bank[bk][:, 0:33],
                dcb33.ap()[0:33, 128 * tt4:128 * (tt4 + 1)], c["id_f"][0:33, 0:33]))
            k.wait("vector", tt3)
            vt3 = k.op("vector", lambda v, tt4=tt4, bk=bk: v.tensor_copy(
                dcb_tm[:, tt4, :], k.bank[bk][:, 0:33]))
            k.bank_free[bk] = vt3
        k.fence("vector")
        k.op("vector", lambda v: v.tensor_copy(sps[:], dcb_tm.ap()[:, :, 0]))
        k.fence("vector")
        k.wait("scalar", k.now("vector"))
        k.op("scalar", lambda sc_: sc_.activation(delta[:], sps[:], AF.Abs))
        k.fence("scalar")
        k.op("scalar", lambda sc_: sc_.activation(delta[:], delta[:], AF.Exp, scale=-1.0))
        k.fence("scalar")
        sd = k.op("scalar", lambda sc_: sc_.activation(delta[:], delta[:], AF.Ln, bias=1.0))
        k.wait("vector", sd)
        k.op("vector", lambda v: v.tensor_scalar(sps[:], sps[:], 0.0, None, ALU.max))
        k.fence("vector")
        vdl = k.op("vector", lambda v: v.tensor_tensor(delta[:], delta[:], sps[:], ALU.add))
        k.fence("vector")
        vdn = k.op("vector", lambda v: v.tensor_scalar_mul(delta[:], delta[:], -1.0))
        vbc = k.op("vector", lambda v: v.tensor_tensor(
            bc_tm.ap(), dcb_tm.ap()[:, :, 1:17], dcb_tm.ap()[:, :, 17:33], ALU.mult))
        k.fence("vector")
        for ic in range(16):
            k.wait("sync", k.now("tensor"))
            la = k.dma("sync", asl[:], alog_d[:, 2048 * ic:2048 * (ic + 1)])
            k.wait("tensor", la)
            for qq in range(4):
                bk = 4 + qq % 2
                k.use_bank(bk)
                ta = k.op("tensor", lambda t, qq=qq, bk=bk: t.matmul(
                    k.bank[bk][:, :], c["ones1_f"][0:1, 0:128],
                    asl[0:1, 512 * qq:512 * (qq + 1)], start=True, stop=True))
                k.wait("scalar", ta)
                sa = k.op("scalar", lambda sc_, qq=qq, bk=bk: sc_.activation(
                    aexp[:, 512 * qq:512 * (qq + 1)], k.bank[bk][:, :], AF.Exp))
                k.bank_free[bk] = sa
            ld = k.dma("scalar", dsl[:, 0:128] if ic == 0 else dsl[:, 0:128], dvec_d[:, 128 * ic:128 * (ic + 1)]) if False else None
            k.use_bank(5)
            td = k.op("tensor", lambda t, ic=ic: t.matmul(
                k.bank[5][:, 0:128], c["ones1_f"][0:1, 0:128],
                dsl[0:1, 128 * ic:128 * (ic + 1)], start=True, stop=True))
            k.wait("vector", td)
            vD = k.op("vector", lambda v: v.tensor_copy(db[:], k.bank[5][:, 0:128]))
            k.bank_free[5] = vD
            for tt4 in range(4):
                k.wait("scalar", k.now("vector"))
                se = k.op("scalar", lambda sc_, tt4=tt4: sc_.activation(
                    wrk[:], aexp[:], AF.Exp, scale=delta[:, tt4:tt4 + 1]))
                k.wait("vector", se)
                vpr = k.op("vector", lambda v, tt4=tt4: v.tensor_tensor(
                    wrk.ap().rearrange("p (a b) -> p a b", b=16),
                    wrk.ap().rearrange("p (a b) -> p a b", b=16),
                    bc_tm.ap()[:, tt4, :].unsqueeze(1).broadcast_to([128, 128, 16]),
                    ALU.mult))
                k.fence("vector")
                vrd = k.op("vector", lambda v: v.reduce_sum(
                    y1.ap(), wrk.ap().rearrange("p (a b) -> p a b", b=16), axis=AX.X))
                k.fence("vector")
                k.op("vector", lambda v: v.tensor_tensor(y1.ap(), y1.ap(), db.ap(), ALU.add))
                k.fence("vector")
                vy2 = k.op("vector", lambda v, tt4=tt4, ic=ic: v.tensor_tensor(
                    y2t[:, tt4, 128 * ic:128 * (ic + 1)], y1.ap(),
                    xct.ap()[:, tt4, 128 * ic:128 * (ic + 1)], ALU.mult))
        if DEBUG_DUMP and p == 0:
            dxh = k.dtmp("t_dbgxh", [128, 16, PT + 3], BF16)
            dxc = k.dtmp("t_dbgxc", [128, 16, PT], BF16)
            ddcb = k.dtmp("t_dbgdcb", [128, 4, 33], F32)
            ddel = k.dtmp("t_dbgdel", [128, 4], F32)
            dy2 = k.dtmp("t_dbgy2", [128, 4, 2048], BF16)
            k.barrier()
            k.dma("sync", dxh, xh.ap())
            k.dma("sync", dxc, xcf.ap())
            k.dma("sync", ddcb, dcb_tm.ap())
            k.dma("sync", ddel, delta.ap())
            k.dma("sync", dy2, y2t.ap())
            dxct = k.dtmp("t_dbgxct", [128, 4, 2048], BF16)
            daex = k.dtmp("t_dbgaex", [128, 2048], F32)
            dbc = k.dtmp("t_dbgbc", [128, 4, 16], F32)
            k.dma("sync", dxct, xct.ap())
            k.dma("sync", daex, aexp.ap())
            k.dma("sync", dbc, bc_tm.ap())
            k.barrier()
        k.wait("tensor", k.now("vector"))
        for tt4 in range(4):
            for ch in range(16):
                bk = 6 + (ch % 2)
                k.use_bank(bk)
                ttb = k.op("tensor", lambda t, tt4=tt4, ch=ch, bk=bk: t.transpose(
                    k.bank_bf(bk)[:, 0:128], y2t.ap()[:, tt4, 128 * ch:128 * (ch + 1)],
                    c["id_bf"][:, :]))
                k.wait("vector", ttb)
                vtb = k.op("vector", lambda v, tt4=tt4, ch=ch, bk=bk: v.tensor_copy(
                    y2f[:, ch, 128 * tt4:128 * (tt4 + 1)], k.bank_bf(bk)[:, 0:128]))
                k.bank_free[bk] = vtb
        for ch in range(16):
            k.wait("scalar", k.now("vector"))
            sz = k.op("scalar", lambda sc_, ch=ch: sc_.activation(
                wrk[:, 0:PT], zf[:, ch, :], AF.Silu))
            k.wait("vector", sz)
            vz = k.op("vector", lambda v, ch=ch: v.tensor_tensor(
                y2f[:, ch, :], y2f[:, ch, :], wrk[:, 0:PT], ALU.mult))
        if DEBUG_DUMP and p == 0:
            dy2f = k.dtmp("t_dbgy2f", [128, 16, PT], BF16)
            dzf = k.dtmp("t_dbgzf", [128, 16, PT], BF16)
            k.barrier()
            k.dma("sync", dy2f, y2f.ap())
            k.dma("sync", dzf, zf.ap())
            k.barrier()
        k.wait("tensor", k.now("vector"))
        prev_st = ZERO_TOK
        for o in range(8):
            poutS.load(o)
            poutS.wait_tensor(o)
            bk = o % 2
            k.use_bank(bk)
            for kc in range(16):
                tw = k.op("tensor", lambda t, o=o, kc=kc, bk=bk: t.matmul(
                    k.bank[bk][:, :], poutS.lhsT(o, kc), y2f[:, kc, :],
                    start=(kc == 0), stop=(kc == 15)))
            poutS.mark_used(o, tw)
            k.wait("scalar", tw)
            k.wait("scalar", prev_st)
            eW = k.op("scalar", lambda sc_, bk=bk: sc_.activation(
                yst[:], k.bank[bk][:, :], AF.Copy))
            k.bank_free[bk] = eW
            k.wait("sync", eW)
            prev_st = k.dma("sync", yattn_d[128 * o:128 * (o + 1), PT * p:PT * (p + 1)],
                            yst[:])
    return k.now("sync")


def build_l1():
    k = KB("l1")
    E = 16
    x_d = k.din("x", [S, D], F32)
    pin_d = k.din("pin", [D, 4096], F32)
    conv_d = k.din("conv", [128, 16, 4], F32)
    pxp_d = k.din("pxp", [2048, 33], F32)
    alog_d = k.din("alog", [1, 32768], F32)
    dvec_d = k.din("dvec", [1, 2048], F32)
    pout_d = k.din("pout", [2048, D], F32)
    wr_d = k.din("wr", [D, E], F32)
    br_d = k.din("br", [1, E], F32)
    w1_d = k.din("w1", [E, D, 4096], F32)
    b1c_d = k.din("b1c", [E, 128, 32], F32)
    w2_d = k.din("w2", [E, 4096, D], F32)
    b2c_d = k.din("b2c", [E, 128, 8], F32)
    y_d = k.dout("y", [HALF, D], F32)

    xnfm_d = k.dtmp("t_xnfm", [8, 128, S], BF16)
    xnf32_d = k.dtmp("t_xnf32", [8, 128, HALF], F32)
    yattn_d = k.dtmp("t_yattn", [D, HALF], F32)
    yffn_d = k.dtmp("t_yffn", [D, HALF], F32)

    c = emit_consts(k)
    k.open_scope()
    xn_fm = k.sb("xn_fm", [128, 8, S], BF16)
    emit_ln_xnfm(k, c, x_d, xn_fm, nt=S // 128, f32fm_d=xnf32_d)
    k.barrier()
    k.dma("sync", xnfm_d.transpose([1, 0, 2]), xn_fm.ap())
    k.close_scope()
    k.open_scope()
    emit_moe_dense(k, c, "m1", xnfm_d, xnf32_d, yffn_d, E, 0,
                   "softmax_top2", AF.Relu, wr_d, br_d, w1_d, b1c_d, w2_d, b2c_d)
    k.close_scope()
    k.open_scope()
    emit_l1_attn(k, c, xnfm_d, pin_d, conv_d, pxp_d, alog_d, dvec_d, pout_d, yattn_d)
    k.close_scope()
    k.open_scope()
    emit_residual_out(k, c, x_d, [(yattn_d, "fm"), (yffn_d, "fm")], y_d)
    k.close_scope()
    return k.finish()


def _prep_l1(inputs, core):
    x_eff, m = _prep_common_x(inputs, core)
    A = np.ascontiguousarray
    cw = np.asarray(inputs["L1_attn_conv_w"])[:, 0, :]
    b1 = np.asarray(inputs["L1_ffn_b1"])
    b2 = np.asarray(inputs["L1_ffn_b2"])
    return dict(
        x=x_eff,
        pin=A(np.asarray(inputs["L1_attn_in_proj_w"]).T, dtype=np.float32),
        conv=A(cw.reshape(16, 128, 4).transpose(1, 0, 2), dtype=np.float32),
        pxp=A(np.asarray(inputs["L1_attn_x_proj_w"]).T, dtype=np.float32),
        alog=A(np.asarray(inputs["L1_attn_A_log"]).reshape(1, -1), dtype=np.float32),
        dvec=A(np.asarray(inputs["L1_attn_D"]).reshape(1, -1), dtype=np.float32),
        pout=A(np.asarray(inputs["L1_attn_out_proj_w"]).T, dtype=np.float32),
        wr=A(np.asarray(inputs["L1_ffn_Wr"]), dtype=np.float32),
        br=A(np.asarray(inputs["L1_ffn_br"]).reshape(1, 16), dtype=np.float32),
        w1=A(np.asarray(inputs["L1_ffn_W1"]), dtype=np.float32),
        b1c=A(b1.reshape(16, 32, 128).transpose(0, 2, 1), dtype=np.float32),
        w2=A(np.asarray(inputs["L1_ffn_W2"]), dtype=np.float32),
        b2c=A(b2.reshape(16, 8, 128).transpose(0, 2, 1), dtype=np.float32),
    )


_BUILDERS[1] = (build_l1, _prep_l1)
